# version 11
# baseline (speedup 1.0000x reference)
import math
import contextlib
import numpy as np

import concourse.bass as bass
import concourse.mybir as mybir
from concourse.bass_utils import run_bass_kernel_spmd

F32 = mybir.dt.float32
BF16 = mybir.dt.bfloat16
AF = mybir.ActivationFunctionType
ALU = mybir.AluOpType
AX = mybir.AxisListType

B = 4
S_FULL = 4096
S_Q = 2048
D = 2048
LAT = 512
H = 16
HD = 128
PD = 64
NE = 16
NI = 768
SCALE = 1.0 / math.sqrt(HD + PD)
LN_EPS = 1e-5
KT_N = S_FULL // 128
P1_CH = 256
VSTR = 129


class Prog:
    ENGS = ("tensor", "vector", "scalar", "gpsimd", "sync")
    SEMS = ("T", "V", "S", "DG", "DS")

    def __init__(self):
        self.stream = {e: [] for e in self.ENGS}
        self.cnt = {s: 0 for s in self.SEMS}

    def op(self, eng, fn, waits=(), inc=None):
        for (s, v) in waits:
            assert v <= self.cnt[s], f"wait on future inc {s}:{v} > {self.cnt[s]}"
        newv = None
        if inc is not None:
            self.cnt[inc] += 16 if inc in ("DG", "DS") else 1
            newv = self.cnt[inc]
        self.stream[eng].append(
            (fn, tuple((s, v) for (s, v) in waits if v > 0), inc))
        return newv

    def replay(self, eng_name, eng, sems):
        lastw = {}
        for fn, waits, inc in self.stream[eng_name]:
            for (s, v) in waits:
                if lastw.get(s, 0) < v:
                    eng.wait_ge(sems[s], v)
                    lastw[s] = v
            inst = fn(eng)
            if inc is not None:
                inst.then_inc(sems[inc], 16 if inc in ("DG", "DS") else 1)


def build_nc(dbg=None):
    nc = bass.Bass("TRN2", target_bir_lowering=False)
    P = Prog()

    d_xT = nc.dram_tensor("xT", [D, S_FULL], F32, kind="ExternalInput").ap()
    d_xqT = nc.dram_tensor("xqT", [D, S_Q], F32, kind="ExternalInput").ap()
    d_xq = nc.dram_tensor("xq", [S_Q, D], F32, kind="ExternalInput").ap()
    d_wdkv = nc.dram_tensor("wdkv", [D, LAT], F32, kind="ExternalInput").ap()
    d_wdq = nc.dram_tensor("wdq", [D, LAT], F32, kind="ExternalInput").ap()
    d_wkr = nc.dram_tensor("wkr", [D, PD], F32, kind="ExternalInput").ap()
    d_wukh = nc.dram_tensor("wukh", [H, LAT, HD], F32, kind="ExternalInput").ap()
    d_wuvh = nc.dram_tensor("wuvh", [H, LAT, HD], F32, kind="ExternalInput").ap()
    d_wuqh = nc.dram_tensor("wuqh", [H, LAT, HD], F32, kind="ExternalInput").ap()
    d_wqrh = nc.dram_tensor("wqrh", [H, LAT, PD], F32, kind="ExternalInput").ap()
    d_wo = nc.dram_tensor("wo", [D, D], F32, kind="ExternalInput").ap()
    d_cosk = nc.dram_tensor("cosk", [PD, S_FULL], F32, kind="ExternalInput").ap()
    d_sink = nc.dram_tensor("sink", [PD, S_FULL], F32, kind="ExternalInput").ap()
    d_cosq = nc.dram_tensor("cosq", [PD, S_Q], F32, kind="ExternalInput").ap()
    d_sinq = nc.dram_tensor("sinq", [PD, S_Q], F32, kind="ExternalInput").ap()
    d_idf = nc.dram_tensor("idf", [128, 128], F32, kind="ExternalInput").ap()
    d_rwt = nc.dram_tensor("rwt", [D, NE], F32, kind="ExternalInput").ap()
    d_ebias = nc.dram_tensor("ebias_bc", [128, NE], F32, kind="ExternalInput").ap()
    d_gupc = nc.dram_tensor("gupc", [NE + 1, 6, D, 256], F32, kind="ExternalInput").ap()
    d_dwnt = nc.dram_tensor("dwnt", [NE + 1, NI, D], F32, kind="ExternalInput").ap()
    d_out = nc.dram_tensor("out", [S_Q, D], F32, kind="ExternalOutput").ap()
    d_dbg = None
    if dbg is not None:
        d_dbg = nc.dram_tensor("dbg", list(dbg[1]), F32, kind="ExternalOutput").ap()

    es = contextlib.ExitStack()
    with es:
        AB = es.enter_context(nc.sbuf_tensor("AB", [128, H * S_Q], BF16))
        IDB = es.enter_context(nc.sbuf_tensor("IDB", [128, 128], BF16))
        IDF = es.enter_context(nc.sbuf_tensor("IDF", [128, 128], F32))
        ST = es.enter_context(nc.sbuf_tensor("ST", [128, 32], F32))
        STAT = [es.enter_context(nc.sbuf_tensor(f"STT{i}", [128, 1], F32))
                for i in range(12)]
        BJ = [es.enter_context(nc.sbuf_tensor(f"BJ{j}", [128, 1], F32))
              for j in range(NE)]
        EBB = es.enter_context(nc.sbuf_tensor("EBB", [128, NE], F32))
        SCO = es.enter_context(nc.sbuf_tensor("SCO", [128, 16 * NE], F32))
        CMB = es.enter_context(nc.sbuf_tensor("CMB", [128, 16 * NE], F32))
        TK0 = es.enter_context(nc.sbuf_tensor("TK0", [128, NE], F32))
        TK1 = es.enter_context(nc.sbuf_tensor("TK1", [128, NE], F32))
        TK2 = es.enter_context(nc.sbuf_tensor("TK2", [128, NE], F32))

        pA = es.enter_context(nc.psum_tensor("pA", [128, 512], F32))
        pB = es.enter_context(nc.psum_tensor("pB", [128, 512], F32))
        pY0 = es.enter_context(nc.psum_tensor("pY0", [128, 1024], F32))
        pY1 = es.enter_context(nc.psum_tensor("pY1", [128, 1024], F32))
        pT = es.enter_context(nc.psum_tensor("pT", [128, 1024], BF16))
        pSM = es.enter_context(nc.psum_tensor("pSM", [128, 512], F32))

        sems = {s: es.enter_context(nc.semaphore("sem" + s)) for s in Prog.SEMS}

        P.op("gpsimd", lambda g: g.dma_start(out=IDB[:], in_=d_idf[:]), inc="DG")
        P.op("sync", lambda g: g.dma_start(out=IDF[:], in_=d_idf[:]), inc="DS")
        P.op("sync", lambda g: g.dma_start(out=EBB[:], in_=d_ebias[:]), inc="DS")
        ds_setup = P.cnt["DS"]

        s1 = contextlib.ExitStack()
        with s1:
            CK = s1.enter_context(nc.sbuf_tensor("CK", [128, 4 * S_FULL], BF16))
            CQ = s1.enter_context(nc.sbuf_tensor("CQ", [128, 4 * S_Q], BF16))
            KRB = s1.enter_context(nc.sbuf_tensor("KRB", [128, S_FULL], BF16))
            CSK = s1.enter_context(nc.sbuf_tensor("CSK", [PD, S_FULL], BF16))
            SNK = s1.enter_context(nc.sbuf_tensor("SNK", [PD, S_FULL], BF16))
            CSQ = s1.enter_context(nc.sbuf_tensor("CSQ", [PD, S_Q], BF16))
            SNQ = s1.enter_context(nc.sbuf_tensor("SNQ", [PD, S_Q], BF16))

            for dst, src in ((CSK, d_cosk), (SNK, d_sink), (CSQ, d_cosq), (SNQ, d_sinq)):
                P.op("gpsimd", lambda g, dst=dst, src=src: g.dma_start(out=dst[:], in_=src[:]),
                     inc="DG")
            dg_tab = P.cnt["DG"]
            P.op("vector", lambda v: v.memset(KRB[64:128, :], 0.0))

            s1a = contextlib.ExitStack()
            with s1a:
                XC = s1a.enter_context(nc.sbuf_tensor("XC", [128, 2 * 16 * P1_CH], BF16))
                WDKV = s1a.enter_context(nc.sbuf_tensor("WDKV", [128, 16 * LAT], BF16))
                WDQ = s1a.enter_context(nc.sbuf_tensor("WDQ", [128, 16 * LAT], BF16))
                WKR = s1a.enter_context(nc.sbuf_tensor("WKR", [128, 16 * PD], BF16))
                KRP = s1a.enter_context(nc.sbuf_tensor("KRP", [PD, 2 * P1_CH], F32))
                RTK = s1a.enter_context(nc.sbuf_tensor("RTK", [PD, P1_CH], BF16))

                for s in range(16):
                    P.op("gpsimd", lambda g, s=s: g.dma_start(
                        out=WDKV[:, s * LAT:(s + 1) * LAT],
                        in_=d_wdkv[s * 128:(s + 1) * 128, :]), inc="DG")
                for s in range(16):
                    P.op("gpsimd", lambda g, s=s: g.dma_start(
                        out=WDQ[:, s * LAT:(s + 1) * LAT],
                        in_=d_wdq[s * 128:(s + 1) * 128, :]), inc="DG")
                for s in range(16):
                    P.op("gpsimd", lambda g, s=s: g.dma_start(
                        out=WKR[:, s * PD:(s + 1) * PD],
                        in_=d_wkr[s * 128:(s + 1) * 128, :]), inc="DG")
                dg_w1 = P.cnt["DG"]

                xT3 = d_xT.rearrange("(s p) t -> p s t", p=128)
                xqT3 = d_xqT.rearrange("(s p) t -> p s t", p=128)

                rot = [(pA, 0), (pB, 0), (pY0, 0), (pY0, 512), (pY1, 0), (pY1, 512)]
                rot_free = [0] * 6
                rot_i = [0]

                def mm_group(nmm, lhs_fn, rhs_fn, nparts, ncols, evacs, xwaits=()):
                    bk = rot_i[0] % 6
                    rot_i[0] += 1
                    pt, off = rot[bk]
                    pap = pt[0:nparts, off:off + ncols]
                    w0 = [("V", rot_free[bk])] + list(xwaits)
                    tv = None
                    for k in range(nmm):
                        tv = P.op("tensor", lambda t, k=k, pap=pap: t.matmul(
                            pap, lhs_fn(k), rhs_fn(k),
                            start=(k == 0), stop=(k == nmm - 1)),
                            waits=w0 if k == 0 else (),
                            inc="T" if k == nmm - 1 else None)
                    vv = None
                    for i, ev in enumerate(evacs):
                        vv = P.op("vector", lambda v, ev=ev, pap=pap: ev(v, pap),
                                  waits=[("T", tv)] if i == 0 else (), inc="V")
                    rot_free[bk] = vv
                    return vv

                dg_chunk = {}
                t_after_chunk = {}
                for c in range(16):
                    j = c % 2
                    w = [("DG", dg_w1)] if c < 2 else [("T", t_after_chunk[c - 2])]
                    for sg in range(4):
                        dg_chunk[c] = P.op("gpsimd", lambda g, c=c, j=j, sg=sg: g.dma_start(
                            out=XC[:, (j * 16 + sg * 4) * P1_CH:(j * 16 + (sg + 1) * 4) * P1_CH],
                            in_=xT3[:, sg * 4:(sg + 1) * 4, c * P1_CH:(c + 1) * P1_CH]),
                            waits=w if sg == 0 else (), inc="DG")
                    xw = [("DG", dg_chunk[c])]

                    def xc(s, j=j):
                        return XC[:, (j * 16 + s) * P1_CH:(j * 16 + s + 1) * P1_CH]

                    for m in range(4):
                        mm_group(16,
                                 lambda s, m=m: WDKV[:, s * LAT + m * 128:s * LAT + (m + 1) * 128],
                                 xc, 128, P1_CH,
                                 [lambda v, pap, m=m, c=c: v.tensor_copy(
                                     CK[:, m * S_FULL + c * P1_CH:m * S_FULL + (c + 1) * P1_CH],
                                     pap)],
                                 xwaits=xw)
                    mm_group(16, lambda s: WKR[:, s * PD:(s + 1) * PD], xc, PD, P1_CH,
                             [lambda v, pap, j=j: v.tensor_copy(
                                 KRP[:, j * P1_CH:(j + 1) * P1_CH], pap)])
                    t_after_chunk[c] = P.cnt["T"]
                    P.op("vector", lambda v, j=j: v.tensor_copy(
                        RTK[0:32, :], KRP[32:64, j * P1_CH:(j + 1) * P1_CH]),
                        waits=[("DG", dg_tab)])
                    P.op("vector", lambda v, j=j: v.tensor_copy(
                        RTK[32:64, :], KRP[0:32, j * P1_CH:(j + 1) * P1_CH]))
                    P.op("vector", lambda v, j=j, c=c: v.tensor_tensor(
                        KRP[:, j * P1_CH:(j + 1) * P1_CH], KRP[:, j * P1_CH:(j + 1) * P1_CH],
                        CSK[:, c * P1_CH:(c + 1) * P1_CH], op=ALU.mult))
                    P.op("vector", lambda v, c=c: v.tensor_tensor(
                        RTK[:], RTK[:], SNK[:, c * P1_CH:(c + 1) * P1_CH], op=ALU.mult))
                    P.op("vector", lambda v, j=j, c=c: v.tensor_tensor(
                        KRB[0:PD, c * P1_CH:(c + 1) * P1_CH], KRP[:, j * P1_CH:(j + 1) * P1_CH],
                        RTK[:], op=ALU.add), inc="V")

                for c in range(8):
                    j = c % 2
                    for sg in range(4):
                        dg_chunk[16 + c] = P.op("gpsimd", lambda g, c=c, j=j, sg=sg: g.dma_start(
                            out=XC[:, (j * 16 + sg * 4) * P1_CH:(j * 16 + (sg + 1) * 4) * P1_CH],
                            in_=xqT3[:, sg * 4:(sg + 1) * 4, c * P1_CH:(c + 1) * P1_CH]),
                            waits=[("T", t_after_chunk[14 + c])] if sg == 0 else (), inc="DG")
                    xw = [("DG", dg_chunk[16 + c])]

                    def xc(s, j=j):
                        return XC[:, (j * 16 + s) * P1_CH:(j * 16 + s + 1) * P1_CH]

                    for m in range(4):
                        mm_group(16,
                                 lambda s, m=m: WDQ[:, s * LAT + m * 128:s * LAT + (m + 1) * 128],
                                 xc, 128, P1_CH,
                                 [lambda v, pap, m=m, c=c: v.tensor_copy(
                                     CQ[:, m * S_Q + c * P1_CH:m * S_Q + (c + 1) * P1_CH], pap)],
                                 xwaits=xw)
                    t_after_chunk[16 + c] = P.cnt["T"]
                v_p1_done = P.cnt["V"]
                t_p1_done = P.cnt["T"]

                if dbg is not None and dbg[0] == "ckv":
                    P.op("gpsimd", lambda g: g.dma_start(out=d_dbg[:], in_=CK[:]),
                         waits=[("V", v_p1_done)], inc="DG")

            s1b = contextlib.ExitStack()
            with s1b:
                KT = s1b.enter_context(nc.sbuf_tensor("KT", [128, S_FULL], BF16))
                VV = s1b.enter_context(nc.sbuf_tensor("VV", [128, KT_N * VSTR], BF16))
                QH = s1b.enter_context(nc.sbuf_tensor("QH", [128, S_Q], BF16))
                EXPS = s1b.enter_context(nc.sbuf_tensor("EXPS", [128, 1024], BF16))
                QPRE = s1b.enter_context(nc.sbuf_tensor("QPRE", [PD, S_Q], F32))
                RTQ = s1b.enter_context(nc.sbuf_tensor("RTQ", [PD, S_Q], BF16))
                QRB = s1b.enter_context(nc.sbuf_tensor("QRB", [128, S_Q], BF16))
                WUKh = s1b.enter_context(nc.sbuf_tensor("WUKh", [128, 2 * 512], BF16))
                WUVh = s1b.enter_context(nc.sbuf_tensor("WUVh", [128, 2 * 512], BF16))
                WUQh = s1b.enter_context(nc.sbuf_tensor("WUQh", [128, 2 * 512], BF16))
                WQRh = s1b.enter_context(nc.sbuf_tensor("WQRh", [128, 2 * 256], BF16))
                OBUF = s1b.enter_context(nc.sbuf_tensor("OBUF", [128, 512], BF16))

                for kt in range(KT_N):
                    P.op("vector", lambda v, kt=kt: v.memset(
                        VV[:, kt * VSTR + 128:kt * VSTR + 129], 1.0))
                P.op("vector", lambda v: v.memset(QRB[64:128, :], 0.0))

                t_prep_done = {}
                dg_head = {}

                def head_w_dma(h):
                    jw = h % 2
                    w = ([("T", t_prep_done[h - 2])] if h >= 2
                         else [("T", t_p1_done), ("V", v_p1_done)])
                    for dst, src, wd in ((WUKh, d_wukh, HD), (WUVh, d_wuvh, HD),
                                         (WUQh, d_wuqh, HD), (WQRh, d_wqrh, PD)):
                        dg_head[h] = P.op("gpsimd", lambda g, dst=dst, src=src, wd=wd, jw=jw, h=h:
                                          g.dma_start(
                            out=dst[:, jw * 4 * wd:(jw + 1) * 4 * wd],
                            in_=src[h].rearrange("(s p) t -> p s t", p=128)),
                            waits=w, inc="DG")

                head_w_dma(0)
                head_w_dma(1)

                exp_cnt = {}
                v_norm = 0
                v_ot = 0
                for h in range(H):
                    jw = h % 2
                    dgw = [("DG", dg_head[h])]
                    for n in range(4):
                        mm_group(4,
                                 lambda m, jw=jw: WQRh[:, jw * 256 + m * PD:jw * 256 + (m + 1) * PD],
                                 lambda m, n=n: CQ[:, m * S_Q + n * 512:m * S_Q + (n + 1) * 512],
                                 PD, 512,
                                 [lambda v, pap, n=n: v.tensor_copy(
                                     QPRE[:, n * 512:(n + 1) * 512], pap)],
                                 xwaits=dgw + [("V", v_ot)])
                    P.op("vector", lambda v: v.tensor_copy(RTQ[0:32, :], QPRE[32:64, :]),
                         waits=[("DG", dg_tab)])
                    P.op("vector", lambda v: v.tensor_copy(RTQ[32:64, :], QPRE[0:32, :]))
                    P.op("vector", lambda v: v.tensor_tensor(QPRE[:], QPRE[:], CSQ[:], op=ALU.mult))
                    P.op("vector", lambda v: v.tensor_tensor(RTQ[:], RTQ[:], SNQ[:], op=ALU.mult))
                    v_rope = P.op("vector", lambda v: v.tensor_tensor(
                        QRB[0:PD, :], QPRE[:], RTQ[:], op=ALU.add), inc="V")

                    for n in range(8):
                        mm_group(4,
                                 lambda m, jw=jw: WUKh[:, jw * 512 + m * 128:jw * 512 + (m + 1) * 128],
                                 lambda m, n=n: CK[:, m * S_FULL + n * 512:m * S_FULL + (n + 1) * 512],
                                 128, 512,
                                 [lambda v, pap, n=n: v.tensor_copy(
                                     KT[:, n * 512:(n + 1) * 512], pap)])
                    for g4 in range(8):
                        bk = rot_i[0] % 6
                        rot_i[0] += 1
                        pt_, off = rot[bk]
                        tv = None
                        for q in range(4):
                            tt = g4 * 4 + q
                            for m in range(4):
                                tv = P.op("tensor", lambda t, q=q, m=m, tt=tt, pt_=pt_, off=off, jw=jw:
                                          t.matmul(
                                    pt_[:, off + q * 128:off + (q + 1) * 128],
                                    CK[:, m * S_FULL + tt * 128:m * S_FULL + (tt + 1) * 128],
                                    WUVh[:, jw * 512 + m * 128:jw * 512 + (m + 1) * 128],
                                    start=(m == 0), stop=(m == 3)),
                                    waits=[("V", rot_free[bk])] if (q == 0 and m == 0) else (),
                                    inc="T" if (q == 3 and m == 3) else None)
                        vv = None
                        for q in range(4):
                            tt = g4 * 4 + q
                            vv = P.op("vector", lambda v, q=q, tt=tt, pt_=pt_, off=off: v.tensor_copy(
                                VV[:, tt * VSTR:tt * VSTR + 128],
                                pt_[:, off + q * 128:off + (q + 1) * 128]),
                                waits=[("T", tv)] if q == 0 else (), inc="V")
                        rot_free[bk] = vv
                    for n in range(4):
                        mm_group(4,
                                 lambda m, jw=jw: WUQh[:, jw * 512 + m * 128:jw * 512 + (m + 1) * 128],
                                 lambda m, n=n: CQ[:, m * S_Q + n * 512:m * S_Q + (n + 1) * 512],
                                 128, 512,
                                 [lambda v, pap, n=n: v.tensor_copy(
                                     QH[:, n * 512:(n + 1) * 512], pap)])
                    v_prep = P.cnt["V"]
                    t_prep_done[h] = P.cnt["T"]
                    if h + 2 < H:
                        head_w_dma(h + 2)

                    for qg in range(4):
                        po_t = ((pY0, 0), (pY0, 512), (pY1, 0), (pY1, 512))
                        ts_cnt = {}
                        for kt in range(KT_N):
                            pS = pA if kt % 2 == 0 else pB
                            w0 = []
                            if kt == 0:
                                w0 = [("V", max(v_prep, v_rope))]
                            P.op("tensor", lambda t, kt=kt, qg=qg, pS=pS: t.matmul(
                                pS[:, 0:512], KT[:, kt * 128:(kt + 1) * 128],
                                QH[:, qg * 512:(qg + 1) * 512], start=True, stop=False),
                                waits=w0)
                            ts_cnt[kt] = P.op("tensor", lambda t, kt=kt, qg=qg, pS=pS: t.matmul(
                                pS[:, 0:512], KRB[:, kt * 128:(kt + 1) * 128],
                                QRB[:, qg * 512:(qg + 1) * 512], start=False, stop=True),
                                inc="T")
                            exp_cnt[kt] = P.op("scalar", lambda s, kt=kt: s.activation(
                                EXPS[:, (kt % 2) * 512:(kt % 2 + 1) * 512],
                                (pA if kt % 2 == 0 else pB)[:, 0:512],
                                AF.Exp, scale=SCALE), waits=[("T", ts_cnt[kt])], inc="S")
                            if kt >= 1:
                                ktp = kt - 1
                                tpv = None
                                for qi in range(4):
                                    pot, poo = po_t[qi]
                                    tpv = P.op("tensor", lambda t, ktp=ktp, qi=qi, pot=pot, poo=poo: t.matmul(
                                        pot[:, poo:poo + VSTR],
                                        EXPS[:, (ktp % 2) * 512 + qi * 128:(ktp % 2) * 512 + (qi + 1) * 128],
                                        VV[:, ktp * VSTR:ktp * VSTR + VSTR],
                                        start=(ktp == 0), stop=(ktp == KT_N - 1)),
                                        waits=([("S", exp_cnt[ktp])] + ([("V", v_norm)] if ktp == 0 else []))
                                        if qi == 0 else (),
                                        inc="T" if (ktp == KT_N - 1 and qi == 3) else None)
                        ktp = KT_N - 1
                        tpv = None
                        for qi in range(4):
                            pot, poo = po_t[qi]
                            tpv = P.op("tensor", lambda t, ktp=ktp, qi=qi, pot=pot, poo=poo: t.matmul(
                                pot[:, poo:poo + VSTR],
                                EXPS[:, (ktp % 2) * 512 + qi * 128:(ktp % 2) * 512 + (qi + 1) * 128],
                                VV[:, ktp * VSTR:ktp * VSTR + VSTR],
                                start=False, stop=True),
                                waits=[("S", exp_cnt[ktp])] if qi == 0 else (),
                                inc="T" if qi == 3 else None)
                        fv = None
                        for qi in range(4):
                            pot, poo = po_t[qi]
                            fv = P.op("vector", lambda v, qi=qi, pot=pot, poo=poo: v.reciprocal(
                                STAT[4 + qi][:], pot[:, poo + 128:poo + 129]),
                                waits=[("T", tpv)] if qi == 0 else (),
                                inc="V" if qi == 3 else None)
                        for qi in range(4):
                            pot, poo = po_t[qi]
                            v_norm = P.op("vector", lambda v, qi=qi, pot=pot, poo=poo: v.tensor_scalar_mul(
                                OBUF[:, qi * 128:(qi + 1) * 128],
                                pot[:, poo:poo + 128], STAT[4 + qi][:]),
                                waits=[("V", fv)] if qi == 0 else (), inc="V")
                        ttr = None
                        for qi in range(4):
                            ttr = P.op("tensor", lambda t, qi=qi: t.transpose(
                                pT[:, qi * 128:(qi + 1) * 128], OBUF[:, qi * 128:(qi + 1) * 128],
                                IDB[:]),
                                waits=[("V", v_norm), ("V", v_ot), ("DG", 16)] if qi == 0 else (),
                                inc="T" if qi == 3 else None)
                        for qi in range(4):
                            v_ot = P.op("vector", lambda v, qi=qi, h=h, qg=qg: v.tensor_copy(
                                AB[:, h * S_Q + qg * 512 + qi * 128:h * S_Q + qg * 512 + (qi + 1) * 128],
                                pT[:, qi * 128:(qi + 1) * 128]),
                                waits=[("T", ttr)] if qi == 0 else (), inc="V")
                v_p2_done = P.cnt["V"]
                t_p2_done = P.cnt["T"]
                s_p2_done = P.cnt["S"]

                if dbg is not None and dbg[0] == "ot":
                    P.op("gpsimd", lambda g: g.dma_start(out=d_dbg[:], in_=AB[:]),
                         waits=[("V", v_p2_done)], inc="DG")
                if dbg is not None and dbg[0] in ("kt", "qh", "krb", "qrb", "vv", "exps"):
                    srcb = {"kt": KT, "qh": QH, "krb": KRB, "qrb": QRB, "vv": VV,
                            "exps": EXPS}[dbg[0]]
                    P.op("gpsimd", lambda g, srcb=srcb: g.dma_start(out=d_dbg[:], in_=srcb[:]),
                         waits=[("V", v_p2_done), ("T", t_p2_done), ("S", s_p2_done)], inc="DG")

        s2 = contextlib.ExitStack()
        with s2:
            RT = s2.enter_context(nc.sbuf_tensor("RT", [128, 16 * D], BF16))
            s2a = contextlib.ExitStack()
            with s2a:
                FB = s2a.enter_context(nc.sbuf_tensor("FB", [128, 8192], F32))
                XQ = s2a.enter_context(nc.sbuf_tensor("XQ", [128, 2 * D], F32))
                RWT = s2a.enter_context(nc.sbuf_tensor("RWT", [128, 16 * NE], F32))
                zbuf = lambda: FB[:, 0:2048]
                cbuf = lambda: FB[:, 4096:6144]
                x1tf = lambda: FB[:, 6144:8192]

                t_wo_dma = None
                for s in range(16):
                    t_wo_dma = P.op("gpsimd", lambda g, s=s: g.dma_start(
                        out=RT[:, s * D:(s + 1) * D], in_=d_wo[s * 128:(s + 1) * 128, :]),
                        waits=[("T", t_p2_done), ("V", v_p2_done), ("S", s_p2_done)]
                        if s == 0 else (), inc="DG")
                dg_wo = P.cnt["DG"]
                for s in range(16):
                    P.op("sync", lambda g, s=s: g.dma_start(
                        out=RWT[:, s * NE:(s + 1) * NE], in_=d_rwt[s * 128:(s + 1) * 128, :]),
                        waits=[("T", t_p2_done), ("V", v_p2_done)] if s == 0 else (), inc="DS")
                ds_rwt = P.cnt["DS"]

                ds_xq = {}
                v_ln = {}
                v_evz = 0
                v_x1 = 0
                s_rt = 0

                def xq_dma(t):
                    jx = t % 2
                    w = ([("V", v_ln[t - 2])] if t >= 2
                         else [("T", t_p2_done), ("V", v_p2_done)])
                    ds_xq[t] = P.op("sync", lambda g, t=t, jx=jx: g.dma_start(
                        out=XQ[:, jx * D:(jx + 1) * D], in_=d_xq[t * 128:(t + 1) * 128, :]),
                        waits=w, inc="DS")
                zb = lambda t: FB[:, (t % 2) * 2048:(t % 2) * 2048 + 2048]
                xq_dma(0)
                xq_dma(1)

                def emit_wo_ln(t):
                    nonlocal v_evz
                    jx = t % 2
                    for dch in range(4):
                        pW = pA if dch % 2 == 0 else pB
                        tv = None
                        for s in range(16):
                            w0 = []
                            if t == 0 and dch == 0 and s == 0:
                                w0 = [("DG", dg_wo)]
                            elif s == 0:
                                w0 = [("V", v_evz)]
                            tv = P.op("tensor", lambda tn, s=s, dch=dch, t=t, pW=pW: tn.matmul(
                                pW[:, 0:512],
                                AB[:, s * S_Q + t * 128:s * S_Q + (t + 1) * 128],
                                RT[:, s * D + dch * 512:s * D + (dch + 1) * 512],
                                start=(s == 0), stop=(s == 15)),
                                waits=w0, inc="T" if s == 15 else None)
                        v_evz = P.op("vector", lambda v, dch=dch, jx=jx, pW=pW, t=t: v.tensor_tensor(
                            zb(t)[:, dch * 512:(dch + 1) * 512], pW[:, 0:512],
                            XQ[:, jx * D + dch * 512:jx * D + (dch + 1) * 512], op=ALU.add),
                            waits=[("T", tv), ("DS", ds_xq[t])], inc="V")
                    if dbg is not None and dbg[0] == "z0" and t == 0:
                        P.op("sync", lambda g, t=t: g.dma_start(out=d_dbg[:], in_=zb(t)),
                             waits=[("V", v_evz)], inc="DS")
                    sv = P.op("scalar", lambda s, jx=jx, t=t: s.activation(
                        XQ[:, jx * D:(jx + 1) * D], zb(t), AF.Square,
                        accum_out=STAT[2][:]),
                        waits=[("V", v_evz)], inc="S")
                    fv = P.op("vector", lambda v, t=t: v.reduce_sum(
                        STAT[0][:], zb(t), axis=AX.X), inc="V")
                    fv = P.op("vector", lambda v: v.tensor_scalar(
                        STAT[1][:], STAT[0][:], 1.0 / D, None, op0=ALU.mult),
                        waits=[("V", fv)], inc="V")
                    vc = P.op("vector", lambda v, t=t: v.tensor_scalar_sub(
                        cbuf(), zb(t), STAT[1][:]), waits=[("V", fv)], inc="V")
                    fv = P.op("vector", lambda v: v.tensor_tensor(
                        STAT[4][:], STAT[1][:], STAT[1][:], op=ALU.mult),
                        waits=[("V", fv)], inc="V")
                    fv = P.op("vector", lambda v: v.scalar_tensor_tensor(
                        STAT[3][:], STAT[2][:], 1.0 / D, STAT[4][:],
                        op0=ALU.mult, op1=ALU.subtract),
                        waits=[("V", fv), ("S", sv)], inc="V")
                    fv = P.op("vector", lambda v: v.tensor_scalar(
                        STAT[3][:], STAT[3][:], 1.0, LN_EPS, op0=ALU.mult, op1=ALU.add),
                        waits=[("V", fv)], inc="V")
                    sv = P.op("scalar", lambda s: s.sqrt(STAT[3][:], STAT[3][:]),
                              waits=[("V", fv)], inc="S")
                    fv = P.op("vector", lambda v: v.reciprocal(STAT[3][:], STAT[3][:]),
                         waits=[("S", sv)], inc="V")
                    v_ln[t] = P.op("vector", lambda v, t=t: v.tensor_scalar_mul(
                        zb(t), cbuf(), STAT[3][:]), waits=[("V", fv)], inc="V")
                    if t + 2 < 16:
                        xq_dma(t + 2)

                tail_sx = {}

                def emit_tail(t):
                    nonlocal v_x1, s_rt
                    s_x1 = tail_sx.get(t - 1, 0)
                    for r in range(4):
                        pf, off = (pY0, 0) if r % 2 == 0 else (pY0, 512)
                        ttr = None
                        for b_ in range(4):
                            f = r * 4 + b_
                            wtr = []
                            if b_ == 0:
                                wtr = [("V", v_ln[t] if r == 0 else v_x1),
                                       ("DS", ds_setup), ("S", s_x1)]
                            ttr = P.op("tensor", lambda tn, f=f, b_=b_, pf=pf, off=off, t=t:
                                       tn.transpose(
                                pf[:, off + b_ * 128:off + (b_ + 1) * 128],
                                zb(t)[:, f * 128:(f + 1) * 128], IDF[:]),
                                waits=wtr, inc="T" if b_ == 3 else None)
                        if r % 2 == 0:
                            s_x1 = P.op("scalar", lambda s, r=r, pf=pf, off=off: s.copy(
                                x1tf()[:, r * 512:(r + 1) * 512], pf[:, off:off + 512]),
                                waits=[("T", ttr)], inc="S")
                            for b_ in range(4):
                                f = r * 4 + b_
                                s_x1 = P.op("scalar", lambda s, f=f, b_=b_, t=t, pf=pf, off=off:
                                            s.copy(
                                    AB[:, f * S_Q + t * 128:f * S_Q + (t + 1) * 128],
                                    pf[:, off + b_ * 128:off + (b_ + 1) * 128]), inc="S")
                        else:
                            v_x1 = P.op("vector", lambda v, r=r, pf=pf, off=off: v.tensor_copy(
                                x1tf()[:, r * 512:(r + 1) * 512], pf[:, off:off + 512]),
                                waits=[("T", ttr)], inc="V")
                            for b_ in range(4):
                                f = r * 4 + b_
                                v_x1 = P.op("vector", lambda v, f=f, b_=b_, t=t, pf=pf, off=off:
                                            v.tensor_copy(
                                    AB[:, f * S_Q + t * 128:f * S_Q + (t + 1) * 128],
                                    pf[:, off + b_ * 128:off + (b_ + 1) * 128]), inc="V")
                        tail_sx[t] = s_x1
                    trm = None
                    for s in range(16):
                        trm = P.op("tensor", lambda tn, s=s: tn.matmul(
                            pSM[:, 0:NE], x1tf()[:, s * 128:(s + 1) * 128],
                            RWT[:, s * NE:(s + 1) * NE], start=(s == 0), stop=(s == 15)),
                            waits=[("V", v_x1), ("DS", ds_rwt), ("S", tail_sx[t])]
                            if s == 0 else (),
                            inc="T" if s == 15 else None)
                    s_rt = P.op("scalar", lambda s, t=t: s.activation(
                        SCO[:, t * NE:(t + 1) * NE], pSM[:, 0:NE], AF.Sigmoid),
                        waits=[("T", trm)], inc="S")
                    fv = P.op("vector", lambda v, t=t: v.tensor_tensor(
                        TK0[:], SCO[:, t * NE:(t + 1) * NE], EBB[:], op=ALU.add),
                        waits=[("S", s_rt), ("DS", ds_setup)], inc="V")
                    for j in range(NE):
                        P.op("vector", lambda v, j=j: v.tensor_copy(
                            BJ[j][:], TK0[:, j:j + 1]),
                            waits=[("V", fv)] if j == 0 else (),
                            inc="V" if j == NE - 1 else None)
                    fv = P.cnt["V"]
                    P.op("vector", lambda v: v.memset(TK1[:], 0.0))
                    for j in range(NE):
                        fj = P.op("vector", lambda v, j=j: v.tensor_single_scalar(
                            TK2[:], TK0[:], BJ[j][:], op=ALU.is_lt),
                            waits=[("V", fv)] if j == 0 else (), inc="V")
                        fa = P.op("vector", lambda v: v.tensor_tensor(
                            TK1[:], TK1[:], TK2[:], op=ALU.add), waits=[("V", fj)],
                            inc="V" if j == NE - 1 else None)
                    if dbg is not None and dbg[0] == "tkd" and t == 15:
                        fvx = P.cnt["V"]
                        for j in range(NE):
                            P.op("sync", lambda g, j=j: g.dma_start(
                                out=d_dbg[:, j:j + 1], in_=BJ[j][:],
                                allow_slow_non_contiguous=True),
                                waits=[("V", fvx)] if j == 0 else (), inc="DS")
                        P.op("sync", lambda g: g.dma_start(
                            out=d_dbg[:, 16:32], in_=TK1[:]), inc="DS")
                        P.op("sync", lambda g: g.dma_start(
                            out=d_dbg[:, 32:48], in_=TK0[:]), inc="DS")
                    wdbg = [("V", fa)]
                    if dbg is not None and dbg[0] == "tkd" and t == 15:
                        wdbg.append(("DS", P.cnt["DS"]))
                    fv = P.op("vector", lambda v: v.tensor_scalar(
                        TK1[:], TK1[:], 4.0, None, op0=ALU.is_lt),
                        waits=wdbg, inc="V")
                    fv = P.op("vector", lambda v, t=t: v.tensor_tensor(
                        TK1[:], TK1[:], SCO[:, t * NE:(t + 1) * NE], op=ALU.mult),
                        waits=[("V", fv)], inc="V")
                    fv = P.op("vector", lambda v: v.reduce_sum(
                        STAT[9][:], TK1[:], axis=AX.X), waits=[("V", fv)], inc="V")
                    fv = P.op("vector", lambda v: v.tensor_scalar(
                        STAT[9][:], STAT[9][:], 1.0, 1e-20, op0=ALU.mult, op1=ALU.add),
                        waits=[("V", fv)], inc="V")
                    fv = P.op("vector", lambda v: v.reciprocal(STAT[9][:], STAT[9][:]),
                              waits=[("V", fv)], inc="V")
                    P.op("vector", lambda v, t=t: v.tensor_scalar_mul(
                        CMB[:, t * NE:(t + 1) * NE], TK1[:], STAT[9][:]),
                        waits=[("V", fv)], inc="V")

                t_wo_last = 0
                for t in range(17):
                    if t >= 1:
                        emit_tail(t - 1)
                    if t < 16:
                        emit_wo_ln(t)
                        t_wo_last = P.cnt["T"]
                v_p3_tiles = P.cnt["V"]
                t_p3_tiles = P.cnt["T"]
                s_p3_tiles = P.cnt["S"]
                v_rt0 = P.op("vector", lambda v: v.memset(RT[:, 0:8 * D], 0.0),
                             waits=[("T", t_wo_last)], inc="V")
                v_rt0 = P.op("vector", lambda v: v.memset(RT[:, 8 * D:16 * D], 0.0), inc="V")

                v_p3_done = P.cnt["V"]
                t_p3_done = P.cnt["T"]
                s_p3_done = P.cnt["S"]

                if dbg is not None and dbg[0] == "x1":
                    P.op("gpsimd", lambda g: g.dma_start(out=d_dbg[:], in_=AB[:]),
                         waits=[("V", v_p3_done)], inc="DG")
                if dbg is not None and dbg[0] == "cmb":
                    P.op("sync", lambda g: g.dma_start(out=d_dbg[:], in_=CMB[:]),
                         waits=[("V", v_p3_done)], inc="DS")
                if dbg is not None and dbg[0] == "sco":
                    P.op("sync", lambda g: g.dma_start(out=d_dbg[:], in_=SCO[:]),
                         waits=[("V", v_p3_done), ("S", s_p3_done)], inc="DS")

            s2b = contextlib.ExitStack()
            with s2b:
                ACT = s2b.enter_context(nc.sbuf_tensor("ACT", [128, 6 * S_Q], BF16))
                DWN = s2b.enter_context(nc.sbuf_tensor("DWN", [128, 6 * D], BF16))
                GUP = s2b.enter_context(nc.sbuf_tensor("GUP", [128, 2 * 16 * 256], BF16))
                SIL = s2b.enter_context(nc.sbuf_tensor("SIL", [128, 512], BF16))

                gup3 = d_gupc.rearrange("e c (s p) i -> e c p s i", p=128)
                dwn3 = d_dwnt.rearrange("e (s p) i -> e p s i", p=128)

                dg_gup = {}
                dg_dwn = {}

                def gup_dma(ci, w=()):
                    e, c = divmod(ci, 6)
                    jg = ci % 2
                    for sg in range(4):
                        dg_gup[ci] = P.op("gpsimd", lambda g, e=e, c=c, jg=jg, sg=sg:
                                          g.dma_start(
                            out=GUP[:, jg * 4096 + sg * 1024:jg * 4096 + (sg + 1) * 1024],
                            in_=gup3[e, c, :, sg * 4:(sg + 1) * 4, :]),
                            waits=list(w) if sg == 0 else (), inc="DG")

                def dwn_dma(e, w=()):
                    dg_dwn[e] = P.op("gpsimd", lambda g, e=e: g.dma_start(
                        out=DWN[:], in_=dwn3[e]),
                        waits=list(w), inc="DG")

                w_region = [("T", t_p3_tiles), ("V", v_p3_tiles), ("S", s_p3_tiles)]
                gup_dma(0, w_region)
                gup_dma(1)
                dwn_dma(0, w_region)

                t_dwn_done = {}
                v_y_bank = {0: 0, 1: 0}
                s_sil = 0
                v_mul = 0
                for e in range(NE + 1):
                    for c in range(6):
                        ci = e * 6 + c
                        jg = ci % 2
                        for n in range(4):
                            tg = None
                            for s in range(16):
                                w0 = []
                                if s == 0:
                                    w0 = [("DG", dg_gup[ci]), ("S", s_sil)]
                                    if ci == 0 and n == 0:
                                        w0.append(("V", v_rt0))
                                tg = P.op("tensor", lambda tn, s=s, n=n, jg=jg: tn.matmul(
                                    pA[:, 0:512],
                                    GUP[:, jg * 4096 + s * 256:jg * 4096 + s * 256 + 128],
                                    AB[:, s * S_Q + n * 512:s * S_Q + (n + 1) * 512],
                                    start=(s == 0), stop=(s == 15)),
                                    waits=w0, inc="T" if s == 15 else None)
                            tu = None
                            for s in range(16):
                                w0 = [("V", v_mul)] if s == 0 else ()
                                tu = P.op("tensor", lambda tn, s=s, n=n, jg=jg: tn.matmul(
                                    pB[:, 0:512],
                                    GUP[:, jg * 4096 + s * 256 + 128:jg * 4096 + (s + 1) * 256],
                                    AB[:, s * S_Q + n * 512:s * S_Q + (n + 1) * 512],
                                    start=(s == 0), stop=(s == 15)),
                                    waits=w0, inc="T" if s == 15 else None)
                            s_sil = P.op("scalar", lambda s: s.activation(
                                SIL[:], pA[:, 0:512], AF.Silu),
                                waits=[("T", tg), ("V", v_mul)], inc="S")
                            v_mul = P.op("vector", lambda v, c=c, n=n: v.tensor_tensor(
                                ACT[:, c * S_Q + n * 512:c * S_Q + (n + 1) * 512],
                                SIL[:], pB[:, 0:512], op=ALU.mult),
                                waits=[("T", tu), ("S", s_sil)], inc="V")
                        if ci + 2 <= (NE + 1) * 6 - 1:
                            gup_dma(ci + 2, [("T", P.cnt["T"])])
                    v_act_done = P.cnt["V"]
                    for t in range(16):
                        for half in range(2):
                            yb = (t * 2 + half) % 2
                            pYx = pY0 if yb == 0 else pY1
                            tv = None
                            for n2 in range(2):
                                for s in range(6):
                                    w0 = []
                                    if n2 == 0 and s == 0:
                                        w0 = [("V", v_y_bank[yb]), ("DG", dg_dwn[e])]
                                        if t == 0 and half == 0:
                                            w0.append(("V", v_act_done))
                                    tv = P.op("tensor", lambda tn, s=s, t=t, half=half, n2=n2, pYx=pYx:
                                              tn.matmul(
                                        pYx[:, n2 * 512:(n2 + 1) * 512],
                                        ACT[:, s * S_Q + t * 128:s * S_Q + (t + 1) * 128],
                                        DWN[:, s * D + half * 1024 + n2 * 512:
                                            s * D + half * 1024 + (n2 + 1) * 512],
                                        start=(s == 0), stop=(s == 5)),
                                        waits=w0, inc="T" if (s == 5 and n2 == 1) else None)
                            if e < NE:
                                v_y_bank[yb] = P.op("vector", lambda v, t=t, half=half, e=e, pYx=pYx:
                                           v.scalar_tensor_tensor(
                                    RT[:, t * D + half * 1024:t * D + (half + 1) * 1024],
                                    pYx[:, 0:1024], CMB[:, t * NE + e:t * NE + e + 1],
                                    RT[:, t * D + half * 1024:t * D + (half + 1) * 1024],
                                    op0=ALU.mult, op1=ALU.add),
                                    waits=[("T", tv)], inc="V")
                            else:
                                v_y_bank[yb] = P.op("vector", lambda v, t=t, half=half, pYx=pYx:
                                           v.tensor_tensor(
                                    RT[:, t * D + half * 1024:t * D + (half + 1) * 1024],
                                    pYx[:, 0:1024],
                                    RT[:, t * D + half * 1024:t * D + (half + 1) * 1024],
                                    op=ALU.add),
                                    waits=[("T", tv)], inc="V")
                    t_dwn_done[e] = P.cnt["T"]
                    if e < NE:
                        dwn_dma(e + 1, [("T", t_dwn_done[e])])
                v_p4_done = P.cnt["V"]
                t_p4_done = P.cnt["T"]

                if dbg is not None and dbg[0] == "rt":
                    P.op("gpsimd", lambda g: g.dma_start(out=d_dbg[:], in_=RT[:]),
                         waits=[("V", v_p4_done)], inc="DG")

            s2c = contextlib.ExitStack()
            with s2c:
                FB2 = s2c.enter_context(nc.sbuf_tensor("FB2", [128, 6144], F32))
                OUTB = s2c.enter_context(nc.sbuf_tensor("OUTB", [128, 2 * D], F32))
                z2 = lambda: FB2[:, 0:2048]
                c2 = lambda: FB2[:, 2048:4096]
                q2 = lambda: FB2[:, 4096:6144]

                ds_out = {}
                v_z = 0
                for t in range(16):
                    jo = t % 2
                    for r in range(2):
                        ttr = None
                        for b_ in range(8):
                            f = r * 8 + b_
                            w0 = []
                            if b_ == 0:
                                w0 = [("V", v_z), ("T", t_p4_done)]
                            ttr = P.op("tensor", lambda tn, f=f, b_=b_, t=t: tn.transpose(
                                pT[:, b_ * 128:(b_ + 1) * 128],
                                AB[:, f * S_Q + t * 128:f * S_Q + (t + 1) * 128], IDB[:]),
                                waits=w0, inc="T" if b_ == 7 else None)
                        v_z = P.op("vector", lambda v, r=r, t=t: v.tensor_tensor(
                            z2()[:, r * 1024:(r + 1) * 1024], pT[:, 0:1024],
                            RT[:, t * D + r * 1024:t * D + (r + 1) * 1024], op=ALU.add),
                            waits=[("T", ttr), ("V", v_p4_done)], inc="V")
                    fv = P.op("vector", lambda v: v.reduce_sum(
                        STAT[0][:], z2(), axis=AX.X), inc="V")
                    fv = P.op("vector", lambda v: v.tensor_scalar(
                        STAT[1][:], STAT[0][:], 1.0 / D, None, op0=ALU.mult),
                        waits=[("V", fv)], inc="V")
                    vc = P.op("vector", lambda v: v.tensor_scalar_sub(
                        c2(), z2(), STAT[1][:]), waits=[("V", fv)], inc="V")
                    sv = P.op("scalar", lambda s: s.activation(
                        q2(), c2(), AF.Square, accum_out=STAT[2][:]),
                        waits=[("V", vc)], inc="S")
                    vv = P.op("vector", lambda v: v.tensor_scalar(
                        STAT[3][:], STAT[2][:], 1.0 / D, LN_EPS, op0=ALU.mult, op1=ALU.add),
                        waits=[("S", sv)], inc="V")
                    sv = P.op("scalar", lambda s: s.sqrt(STAT[3][:], STAT[3][:]),
                              waits=[("V", vv)], inc="S")
                    fv = P.op("vector", lambda v: v.reciprocal(STAT[3][:], STAT[3][:]),
                         waits=[("S", sv)], inc="V")
                    w_ob = ([("DS", ds_out[t - 2])] if t >= 2 else []) + [("V", fv)]
                    v_ob = P.op("vector", lambda v, jo=jo: v.tensor_scalar_mul(
                        OUTB[:, jo * D:(jo + 1) * D], c2(), STAT[3][:]),
                        waits=w_ob, inc="V")
                    ds_out[t] = P.op("sync", lambda g, t=t, jo=jo: g.dma_start(
                        out=d_out[t * 128:(t + 1) * 128, :], in_=OUTB[:, jo * D:(jo + 1) * D]),
                        waits=[("V", v_ob)], inc="DS")

                P.op("sync", lambda g: g.nop(), waits=[("DS", ds_out[15])])

        with nc.Block() as block:
            @block.tensor
            def _(eng):
                P.replay("tensor", eng, sems)

            @block.vector
            def _(eng):
                P.replay("vector", eng, sems)

            @block.scalar
            def _(eng):
                P.replay("scalar", eng, sems)

            @block.gpsimd
            def _(eng):
                P.replay("gpsimd", eng, sems)

            @block.sync
            def _(eng):
                P.replay("sync", eng, sems)

    return nc


def _rope_tables(n):
    invf = (1.0 / (10000.0 ** (np.arange(0, PD, 2, dtype=np.float32) / PD))).astype(np.float32)
    t = np.arange(n, dtype=np.float32)
    th = invf[:, None] * t[None, :]
    cos = np.cos(np.concatenate([th, th], 0)).astype(np.float32)
    sin = np.sin(np.concatenate([th, th], 0)).astype(np.float32)
    sgn = np.ones((PD, 1), np.float32)
    sgn[: PD // 2] = -1.0
    return cos, (sin * sgn).astype(np.float32)


def kernel(L0_attn_hidden_states, L0_attn_W_DKV, L0_attn_W_DQ, L0_attn_W_UK,
           L0_attn_W_UV, L0_attn_W_UQ, L0_attn_W_KR, L0_attn_W_QR, L0_attn_W_O,
           L0_ffn_router_w, L0_ffn_e_bias, L0_ffn_gate_up_proj, L0_ffn_down_proj,
           L0_ffn_shared_gate_w, L0_ffn_shared_up_w, L0_ffn_shared_down_w):
    x = np.asarray(L0_attn_hidden_states, dtype=np.float32)
    asf = lambda a: np.ascontiguousarray(np.asarray(a, dtype=np.float32))

    wdkv = asf(L0_attn_W_DKV)
    wdq = asf(L0_attn_W_DQ)
    wkr = asf(L0_attn_W_KR)
    wukh = asf(np.asarray(L0_attn_W_UK, np.float32).reshape(LAT, H, HD).transpose(1, 0, 2))
    wuvh = asf(np.asarray(L0_attn_W_UV, np.float32).reshape(LAT, H, HD).transpose(1, 0, 2))
    wuqh = asf(np.asarray(L0_attn_W_UQ, np.float32).reshape(LAT, H, HD).transpose(1, 0, 2))
    wqrh = asf(np.asarray(L0_attn_W_QR, np.float32).reshape(LAT, H, PD).transpose(1, 0, 2))
    wo = asf(L0_attn_W_O)
    rwt = asf(np.asarray(L0_ffn_router_w, np.float32).T)
    ebias_bc = asf(np.broadcast_to(np.asarray(L0_ffn_e_bias, np.float32)[None, :], (128, NE)))

    gup = np.asarray(L0_ffn_gate_up_proj, np.float32)
    dwn = np.asarray(L0_ffn_down_proj, np.float32)
    sg = np.asarray(L0_ffn_shared_gate_w, np.float32)
    su = np.asarray(L0_ffn_shared_up_w, np.float32)
    sd = np.asarray(L0_ffn_shared_down_w, np.float32)

    gupc = np.empty((NE + 1, 6, D, 256), np.float32)
    for e in range(NE):
        gT = gup[e, :NI].T
        uT = gup[e, NI:].T
        for c in range(6):
            gupc[e, c, :, :128] = gT[:, c * 128:(c + 1) * 128]
            gupc[e, c, :, 128:] = uT[:, c * 128:(c + 1) * 128]
    sgT = sg.T
    suT = su.T
    for c in range(6):
        gupc[NE, c, :, :128] = sgT[:, c * 128:(c + 1) * 128]
        gupc[NE, c, :, 128:] = suT[:, c * 128:(c + 1) * 128]
    dwnt = np.empty((NE + 1, NI, D), np.float32)
    for e in range(NE):
        dwnt[e] = dwn[e].T
    dwnt[NE] = sd.T

    cosk, sink = _rope_tables(S_FULL)
    idf = np.eye(128, dtype=np.float32)

    nc = build_nc()
    in_maps = []
    for c in range(8):
        b, qh = c // 2, c % 2
        qoff = qh * S_Q
        xT = np.ascontiguousarray(x[b].T)
        xqT = np.ascontiguousarray(xT[:, qoff:qoff + S_Q])
        xq = np.ascontiguousarray(x[b, qoff:qoff + S_Q])
        in_maps.append(dict(
            xT=xT, xqT=xqT, xq=xq, wdkv=wdkv, wdq=wdq, wkr=wkr,
            wukh=wukh, wuvh=wuvh, wuqh=wuqh, wqrh=wqrh, wo=wo,
            cosk=cosk, sink=sink,
            cosq=np.ascontiguousarray(cosk[:, qoff:qoff + S_Q]),
            sinq=np.ascontiguousarray(sink[:, qoff:qoff + S_Q]),
            idf=idf, rwt=rwt, ebias_bc=ebias_bc, gupc=gupc, dwnt=dwnt))

    res = run_bass_kernel_spmd(nc, in_maps, core_ids=list(range(8)))
    out = np.empty((B, S_FULL, D), np.float32)
    for c in range(8):
        b, qh = c // 2, c % 2
        out[b, qh * S_Q:(qh + 1) * S_Q] = res.results[c]["out"]
    return out


# version 12
# speedup vs baseline: 1.1284x; 1.1284x over previous
import math
import contextlib
import numpy as np

import concourse.bass as bass
import concourse.mybir as mybir
from concourse.bass_utils import run_bass_kernel_spmd

F32 = mybir.dt.float32
BF16 = mybir.dt.bfloat16
AF = mybir.ActivationFunctionType
ALU = mybir.AluOpType
AX = mybir.AxisListType

B = 4
S_FULL = 4096
S_Q = 2048
D = 2048
LAT = 512
H = 16
HD = 128
PD = 64
NE = 16
NI = 768
SCALE = 1.0 / math.sqrt(HD + PD)
LN_EPS = 1e-5
KT_N = S_FULL // 128
P1_CH = 256
VSTR = 129


class Prog:
    ENGS = ("tensor", "vector", "scalar", "gpsimd", "sync")
    SEMS = ("T", "V", "S", "DG", "DS")

    def __init__(self):
        self.stream = {e: [] for e in self.ENGS}
        self.cnt = {s: 0 for s in self.SEMS}

    def op(self, eng, fn, waits=(), inc=None):
        for (s, v) in waits:
            assert v <= self.cnt[s], f"wait on future inc {s}:{v} > {self.cnt[s]}"
        newv = None
        if inc is not None:
            self.cnt[inc] += 16 if inc in ("DG", "DS") else 1
            newv = self.cnt[inc]
        self.stream[eng].append(
            (fn, tuple((s, v) for (s, v) in waits if v > 0), inc))
        return newv

    def replay(self, eng_name, eng, sems):
        lastw = {}
        for fn, waits, inc in self.stream[eng_name]:
            for (s, v) in waits:
                if lastw.get(s, 0) < v:
                    eng.wait_ge(sems[s], v)
                    lastw[s] = v
            inst = fn(eng)
            if inc is not None:
                inst.then_inc(sems[inc], 16 if inc in ("DG", "DS") else 1)


def build_nc(dbg=None):
    nc = bass.Bass("TRN2", target_bir_lowering=False)
    P = Prog()

    d_xT = nc.dram_tensor("xT", [D, S_FULL], F32, kind="ExternalInput").ap()
    d_xqT = nc.dram_tensor("xqT", [D, S_Q], F32, kind="ExternalInput").ap()
    d_xq = nc.dram_tensor("xq", [S_Q, D], F32, kind="ExternalInput").ap()
    d_wdkv = nc.dram_tensor("wdkv", [D, LAT], F32, kind="ExternalInput").ap()
    d_wdq = nc.dram_tensor("wdq", [D, LAT], F32, kind="ExternalInput").ap()
    d_wkr = nc.dram_tensor("wkr", [D, PD], F32, kind="ExternalInput").ap()
    d_wukh = nc.dram_tensor("wukh", [H, LAT, HD], F32, kind="ExternalInput").ap()
    d_wuvh = nc.dram_tensor("wuvh", [H, LAT, HD], F32, kind="ExternalInput").ap()
    d_wuqh = nc.dram_tensor("wuqh", [H, LAT, HD], F32, kind="ExternalInput").ap()
    d_wqrh = nc.dram_tensor("wqrh", [H, LAT, PD], F32, kind="ExternalInput").ap()
    d_wo = nc.dram_tensor("wo", [D, D], F32, kind="ExternalInput").ap()
    d_cosk = nc.dram_tensor("cosk", [PD, S_FULL], F32, kind="ExternalInput").ap()
    d_sink = nc.dram_tensor("sink", [PD, S_FULL], F32, kind="ExternalInput").ap()
    d_cosq = nc.dram_tensor("cosq", [PD, S_Q], F32, kind="ExternalInput").ap()
    d_sinq = nc.dram_tensor("sinq", [PD, S_Q], F32, kind="ExternalInput").ap()
    d_idf = nc.dram_tensor("idf", [128, 128], F32, kind="ExternalInput").ap()
    d_rwt = nc.dram_tensor("rwt", [D, NE], F32, kind="ExternalInput").ap()
    d_ebias = nc.dram_tensor("ebias_bc", [128, NE], F32, kind="ExternalInput").ap()
    d_gupc = nc.dram_tensor("gupc", [NE + 1, 6, D, 256], F32, kind="ExternalInput").ap()
    d_dwnt = nc.dram_tensor("dwnt", [NE + 1, NI, D], F32, kind="ExternalInput").ap()
    d_out = nc.dram_tensor("out", [S_Q, D], F32, kind="ExternalOutput").ap()
    d_dbg = None
    if dbg is not None:
        d_dbg = nc.dram_tensor("dbg", list(dbg[1]), F32, kind="ExternalOutput").ap()

    es = contextlib.ExitStack()
    with es:
        AB = es.enter_context(nc.sbuf_tensor("AB", [128, H * S_Q], BF16))
        IDB = es.enter_context(nc.sbuf_tensor("IDB", [128, 128], BF16))
        IDF = es.enter_context(nc.sbuf_tensor("IDF", [128, 128], F32))
        ST = es.enter_context(nc.sbuf_tensor("ST", [128, 32], F32))
        STAT = [es.enter_context(nc.sbuf_tensor(f"STT{i}", [128, 1], F32))
                for i in range(12)]
        BJ = [es.enter_context(nc.sbuf_tensor(f"BJ{j}", [128, 1], F32))
              for j in range(NE)]
        EBB = es.enter_context(nc.sbuf_tensor("EBB", [128, NE], F32))
        SCO = es.enter_context(nc.sbuf_tensor("SCO", [128, 16 * NE], F32))
        CMB = es.enter_context(nc.sbuf_tensor("CMB", [128, 16 * NE], F32))
        TK0 = es.enter_context(nc.sbuf_tensor("TK0", [128, NE], F32))
        TK1 = es.enter_context(nc.sbuf_tensor("TK1", [128, NE], F32))
        TK2 = es.enter_context(nc.sbuf_tensor("TK2", [128, NE], F32))

        pA = es.enter_context(nc.psum_tensor("pA", [128, 512], F32))
        pB = es.enter_context(nc.psum_tensor("pB", [128, 512], F32))
        pY0 = es.enter_context(nc.psum_tensor("pY0", [128, 1024], F32))
        pY1 = es.enter_context(nc.psum_tensor("pY1", [128, 1024], F32))
        pT = es.enter_context(nc.psum_tensor("pT", [128, 1024], BF16))
        pSM = es.enter_context(nc.psum_tensor("pSM", [128, 512], F32))

        sems = {s: es.enter_context(nc.semaphore("sem" + s)) for s in Prog.SEMS}

        P.op("gpsimd", lambda g: g.dma_start(out=IDB[:], in_=d_idf[:]), inc="DG")
        P.op("sync", lambda g: g.dma_start(out=IDF[:], in_=d_idf[:]), inc="DS")
        P.op("sync", lambda g: g.dma_start(out=EBB[:], in_=d_ebias[:]), inc="DS")
        ds_setup = P.cnt["DS"]

        s1 = contextlib.ExitStack()
        with s1:
            CK = s1.enter_context(nc.sbuf_tensor("CK", [128, 4 * S_FULL], BF16))
            CQ = s1.enter_context(nc.sbuf_tensor("CQ", [128, 4 * S_Q], BF16))
            KRB = s1.enter_context(nc.sbuf_tensor("KRB", [128, S_FULL], BF16))
            CSK = s1.enter_context(nc.sbuf_tensor("CSK", [PD, S_FULL], BF16))
            SNK = s1.enter_context(nc.sbuf_tensor("SNK", [PD, S_FULL], BF16))
            CSQ = s1.enter_context(nc.sbuf_tensor("CSQ", [PD, S_Q], BF16))
            SNQ = s1.enter_context(nc.sbuf_tensor("SNQ", [PD, S_Q], BF16))

            for dst, src in ((CSK, d_cosk), (SNK, d_sink), (CSQ, d_cosq), (SNQ, d_sinq)):
                P.op("gpsimd", lambda g, dst=dst, src=src: g.dma_start(out=dst[:], in_=src[:]),
                     inc="DG")
            dg_tab = P.cnt["DG"]
            P.op("vector", lambda v: v.memset(KRB[64:128, :], 0.0))

            s1a = contextlib.ExitStack()
            with s1a:
                XC = s1a.enter_context(nc.sbuf_tensor("XC", [128, 2 * 16 * P1_CH], BF16))
                WDKV = s1a.enter_context(nc.sbuf_tensor("WDKV", [128, 16 * LAT], BF16))
                WDQ = s1a.enter_context(nc.sbuf_tensor("WDQ", [128, 16 * LAT], BF16))
                WKR = s1a.enter_context(nc.sbuf_tensor("WKR", [128, 16 * PD], BF16))
                KRP = s1a.enter_context(nc.sbuf_tensor("KRP", [PD, 2 * P1_CH], F32))
                RTK = s1a.enter_context(nc.sbuf_tensor("RTK", [PD, P1_CH], BF16))

                for s in range(16):
                    P.op("gpsimd", lambda g, s=s: g.dma_start(
                        out=WDKV[:, s * LAT:(s + 1) * LAT],
                        in_=d_wdkv[s * 128:(s + 1) * 128, :]), inc="DG")
                for s in range(16):
                    P.op("gpsimd", lambda g, s=s: g.dma_start(
                        out=WDQ[:, s * LAT:(s + 1) * LAT],
                        in_=d_wdq[s * 128:(s + 1) * 128, :]), inc="DG")
                for s in range(16):
                    P.op("gpsimd", lambda g, s=s: g.dma_start(
                        out=WKR[:, s * PD:(s + 1) * PD],
                        in_=d_wkr[s * 128:(s + 1) * 128, :]), inc="DG")
                dg_w1 = P.cnt["DG"]

                xT3 = d_xT.rearrange("(s p) t -> p s t", p=128)
                xqT3 = d_xqT.rearrange("(s p) t -> p s t", p=128)

                rot = [(pA, 0), (pB, 0), (pY0, 0), (pY0, 512), (pY1, 0), (pY1, 512)]
                rot_free = [0] * 6
                rot_i = [0]

                def mm_group(nmm, lhs_fn, rhs_fn, nparts, ncols, evacs, xwaits=()):
                    bk = rot_i[0] % 6
                    rot_i[0] += 1
                    pt, off = rot[bk]
                    pap = pt[0:nparts, off:off + ncols]
                    w0 = [("V", rot_free[bk])] + list(xwaits)
                    tv = None
                    for k in range(nmm):
                        tv = P.op("tensor", lambda t, k=k, pap=pap: t.matmul(
                            pap, lhs_fn(k), rhs_fn(k),
                            start=(k == 0), stop=(k == nmm - 1)),
                            waits=w0 if k == 0 else (),
                            inc="T" if k == nmm - 1 else None)
                    vv = None
                    for i, ev in enumerate(evacs):
                        vv = P.op("vector", lambda v, ev=ev, pap=pap: ev(v, pap),
                                  waits=[("T", tv)] if i == 0 else (), inc="V")
                    rot_free[bk] = vv
                    return vv

                dg_chunk = {}
                t_after_chunk = {}
                for c in range(16):
                    j = c % 2
                    w = [("DG", dg_w1)] if c < 2 else [("T", t_after_chunk[c - 2])]
                    for sg in range(4):
                        dg_chunk[c] = P.op("gpsimd", lambda g, c=c, j=j, sg=sg: g.dma_start(
                            out=XC[:, (j * 16 + sg * 4) * P1_CH:(j * 16 + (sg + 1) * 4) * P1_CH],
                            in_=xT3[:, sg * 4:(sg + 1) * 4, c * P1_CH:(c + 1) * P1_CH]),
                            waits=w if sg == 0 else (), inc="DG")
                    xw = [("DG", dg_chunk[c])]

                    def xc(s, j=j):
                        return XC[:, (j * 16 + s) * P1_CH:(j * 16 + s + 1) * P1_CH]

                    for m in range(4):
                        mm_group(16,
                                 lambda s, m=m: WDKV[:, s * LAT + m * 128:s * LAT + (m + 1) * 128],
                                 xc, 128, P1_CH,
                                 [lambda v, pap, m=m, c=c: v.tensor_copy(
                                     CK[:, m * S_FULL + c * P1_CH:m * S_FULL + (c + 1) * P1_CH],
                                     pap)],
                                 xwaits=xw)
                    mm_group(16, lambda s: WKR[:, s * PD:(s + 1) * PD], xc, PD, P1_CH,
                             [lambda v, pap, j=j: v.tensor_copy(
                                 KRP[:, j * P1_CH:(j + 1) * P1_CH], pap)])
                    t_after_chunk[c] = P.cnt["T"]
                    P.op("vector", lambda v, j=j: v.tensor_copy(
                        RTK[0:32, :], KRP[32:64, j * P1_CH:(j + 1) * P1_CH]),
                        waits=[("DG", dg_tab)])
                    P.op("vector", lambda v, j=j: v.tensor_copy(
                        RTK[32:64, :], KRP[0:32, j * P1_CH:(j + 1) * P1_CH]))
                    P.op("vector", lambda v, j=j, c=c: v.tensor_tensor(
                        KRP[:, j * P1_CH:(j + 1) * P1_CH], KRP[:, j * P1_CH:(j + 1) * P1_CH],
                        CSK[:, c * P1_CH:(c + 1) * P1_CH], op=ALU.mult))
                    P.op("vector", lambda v, c=c: v.tensor_tensor(
                        RTK[:], RTK[:], SNK[:, c * P1_CH:(c + 1) * P1_CH], op=ALU.mult))
                    P.op("vector", lambda v, j=j, c=c: v.tensor_tensor(
                        KRB[0:PD, c * P1_CH:(c + 1) * P1_CH], KRP[:, j * P1_CH:(j + 1) * P1_CH],
                        RTK[:], op=ALU.add), inc="V")

                for c in range(8):
                    j = c % 2
                    for sg in range(4):
                        dg_chunk[16 + c] = P.op("gpsimd", lambda g, c=c, j=j, sg=sg: g.dma_start(
                            out=XC[:, (j * 16 + sg * 4) * P1_CH:(j * 16 + (sg + 1) * 4) * P1_CH],
                            in_=xqT3[:, sg * 4:(sg + 1) * 4, c * P1_CH:(c + 1) * P1_CH]),
                            waits=[("T", t_after_chunk[14 + c])] if sg == 0 else (), inc="DG")
                    xw = [("DG", dg_chunk[16 + c])]

                    def xc(s, j=j):
                        return XC[:, (j * 16 + s) * P1_CH:(j * 16 + s + 1) * P1_CH]

                    for m in range(4):
                        mm_group(16,
                                 lambda s, m=m: WDQ[:, s * LAT + m * 128:s * LAT + (m + 1) * 128],
                                 xc, 128, P1_CH,
                                 [lambda v, pap, m=m, c=c: v.tensor_copy(
                                     CQ[:, m * S_Q + c * P1_CH:m * S_Q + (c + 1) * P1_CH], pap)],
                                 xwaits=xw)
                    t_after_chunk[16 + c] = P.cnt["T"]
                v_p1_done = P.cnt["V"]
                t_p1_done = P.cnt["T"]

                if dbg is not None and dbg[0] == "ckv":
                    P.op("gpsimd", lambda g: g.dma_start(out=d_dbg[:], in_=CK[:]),
                         waits=[("V", v_p1_done)], inc="DG")

            s1b = contextlib.ExitStack()
            with s1b:
                KT = s1b.enter_context(nc.sbuf_tensor("KT", [128, S_FULL], BF16))
                VV = s1b.enter_context(nc.sbuf_tensor("VV", [128, KT_N * VSTR], BF16))
                QH = s1b.enter_context(nc.sbuf_tensor("QH", [128, S_Q], BF16))
                EXPS = s1b.enter_context(nc.sbuf_tensor("EXPS", [128, 1024], BF16))
                QPRE = s1b.enter_context(nc.sbuf_tensor("QPRE", [PD, S_Q], F32))
                RTQ = s1b.enter_context(nc.sbuf_tensor("RTQ", [PD, S_Q], BF16))
                QRB = s1b.enter_context(nc.sbuf_tensor("QRB", [128, S_Q], BF16))
                WUKh = s1b.enter_context(nc.sbuf_tensor("WUKh", [128, 2 * 512], BF16))
                WUVh = s1b.enter_context(nc.sbuf_tensor("WUVh", [128, 2 * 512], BF16))
                WUQh = s1b.enter_context(nc.sbuf_tensor("WUQh", [128, 2 * 512], BF16))
                WQRh = s1b.enter_context(nc.sbuf_tensor("WQRh", [128, 2 * 256], BF16))
                OBUF = s1b.enter_context(nc.sbuf_tensor("OBUF", [128, 512], BF16))

                for kt in range(KT_N):
                    P.op("vector", lambda v, kt=kt: v.memset(
                        VV[:, kt * VSTR + 128:kt * VSTR + 129], 1.0))
                P.op("vector", lambda v: v.memset(QRB[64:128, :], 0.0))

                t_prep_done = {}
                dg_head = {}

                def head_w_dma(h):
                    jw = h % 2
                    w = ([("T", t_prep_done[h - 2])] if h >= 2
                         else [("T", t_p1_done), ("V", v_p1_done)])
                    for dst, src, wd in ((WUKh, d_wukh, HD), (WUVh, d_wuvh, HD),
                                         (WUQh, d_wuqh, HD), (WQRh, d_wqrh, PD)):
                        dg_head[h] = P.op("gpsimd", lambda g, dst=dst, src=src, wd=wd, jw=jw, h=h:
                                          g.dma_start(
                            out=dst[:, jw * 4 * wd:(jw + 1) * 4 * wd],
                            in_=src[h].rearrange("(s p) t -> p s t", p=128)),
                            waits=w, inc="DG")

                head_w_dma(0)
                head_w_dma(1)

                exp_cnt = {}
                v_norm = 0
                v_ot = 0
                for h in range(H):
                    jw = h % 2
                    dgw = [("DG", dg_head[h])]
                    for n in range(4):
                        mm_group(4,
                                 lambda m, jw=jw: WQRh[:, jw * 256 + m * PD:jw * 256 + (m + 1) * PD],
                                 lambda m, n=n: CQ[:, m * S_Q + n * 512:m * S_Q + (n + 1) * 512],
                                 PD, 512,
                                 [lambda v, pap, n=n: v.tensor_copy(
                                     QPRE[:, n * 512:(n + 1) * 512], pap)],
                                 xwaits=dgw + [("V", v_ot)])
                    P.op("vector", lambda v: v.tensor_copy(RTQ[0:32, :], QPRE[32:64, :]),
                         waits=[("DG", dg_tab)])
                    P.op("vector", lambda v: v.tensor_copy(RTQ[32:64, :], QPRE[0:32, :]))
                    P.op("vector", lambda v: v.tensor_tensor(QPRE[:], QPRE[:], CSQ[:], op=ALU.mult))
                    P.op("vector", lambda v: v.tensor_tensor(RTQ[:], RTQ[:], SNQ[:], op=ALU.mult))
                    v_rope = P.op("vector", lambda v: v.tensor_tensor(
                        QRB[0:PD, :], QPRE[:], RTQ[:], op=ALU.add), inc="V")

                    for n in range(8):
                        mm_group(4,
                                 lambda m, jw=jw: WUKh[:, jw * 512 + m * 128:jw * 512 + (m + 1) * 128],
                                 lambda m, n=n: CK[:, m * S_FULL + n * 512:m * S_FULL + (n + 1) * 512],
                                 128, 512,
                                 [lambda v, pap, n=n: v.tensor_copy(
                                     KT[:, n * 512:(n + 1) * 512], pap)])
                    for g4 in range(8):
                        bk = rot_i[0] % 6
                        rot_i[0] += 1
                        pt_, off = rot[bk]
                        tv = None
                        for q in range(4):
                            tt = g4 * 4 + q
                            for m in range(4):
                                tv = P.op("tensor", lambda t, q=q, m=m, tt=tt, pt_=pt_, off=off, jw=jw:
                                          t.matmul(
                                    pt_[:, off + q * 128:off + (q + 1) * 128],
                                    CK[:, m * S_FULL + tt * 128:m * S_FULL + (tt + 1) * 128],
                                    WUVh[:, jw * 512 + m * 128:jw * 512 + (m + 1) * 128],
                                    start=(m == 0), stop=(m == 3)),
                                    waits=[("V", rot_free[bk])] if (q == 0 and m == 0) else (),
                                    inc="T" if (q == 3 and m == 3) else None)
                        vv = None
                        for q in range(4):
                            tt = g4 * 4 + q
                            vv = P.op("vector", lambda v, q=q, tt=tt, pt_=pt_, off=off: v.tensor_copy(
                                VV[:, tt * VSTR:tt * VSTR + 128],
                                pt_[:, off + q * 128:off + (q + 1) * 128]),
                                waits=[("T", tv)] if q == 0 else (), inc="V")
                        rot_free[bk] = vv
                    for n in range(4):
                        mm_group(4,
                                 lambda m, jw=jw: WUQh[:, jw * 512 + m * 128:jw * 512 + (m + 1) * 128],
                                 lambda m, n=n: CQ[:, m * S_Q + n * 512:m * S_Q + (n + 1) * 512],
                                 128, 512,
                                 [lambda v, pap, n=n: v.tensor_copy(
                                     QH[:, n * 512:(n + 1) * 512], pap)])
                    v_prep = P.cnt["V"]
                    t_prep_done[h] = P.cnt["T"]
                    if h + 2 < H:
                        head_w_dma(h + 2)

                    for qg in range(4):
                        po_t = ((pY0, 0), (pY0, 512), (pY1, 0), (pY1, 512))
                        ts_cnt = {}
                        for kt in range(KT_N):
                            pS = pA if kt % 2 == 0 else pB
                            w0 = []
                            if kt == 0:
                                w0 = [("V", max(v_prep, v_rope))]
                            P.op("tensor", lambda t, kt=kt, qg=qg, pS=pS: t.matmul(
                                pS[:, 0:512], KT[:, kt * 128:(kt + 1) * 128],
                                QH[:, qg * 512:(qg + 1) * 512], start=True, stop=False),
                                waits=w0)
                            ts_cnt[kt] = P.op("tensor", lambda t, kt=kt, qg=qg, pS=pS: t.matmul(
                                pS[:, 0:512], KRB[:, kt * 128:(kt + 1) * 128],
                                QRB[:, qg * 512:(qg + 1) * 512], start=False, stop=True),
                                inc="T")
                            exp_cnt[kt] = P.op("scalar", lambda s, kt=kt: s.activation(
                                EXPS[:, (kt % 2) * 512:(kt % 2 + 1) * 512],
                                (pA if kt % 2 == 0 else pB)[:, 0:512],
                                AF.Exp, scale=SCALE), waits=[("T", ts_cnt[kt])], inc="S")
                            if kt >= 1:
                                ktp = kt - 1
                                tpv = None
                                for qi in range(4):
                                    pot, poo = po_t[qi]
                                    tpv = P.op("tensor", lambda t, ktp=ktp, qi=qi, pot=pot, poo=poo: t.matmul(
                                        pot[:, poo:poo + VSTR],
                                        EXPS[:, (ktp % 2) * 512 + qi * 128:(ktp % 2) * 512 + (qi + 1) * 128],
                                        VV[:, ktp * VSTR:ktp * VSTR + VSTR],
                                        start=(ktp == 0), stop=(ktp == KT_N - 1)),
                                        waits=([("S", exp_cnt[ktp])] + ([("V", v_norm)] if ktp == 0 else []))
                                        if qi == 0 else (),
                                        inc="T" if (ktp == KT_N - 1 and qi == 3) else None)
                        ktp = KT_N - 1
                        tpv = None
                        for qi in range(4):
                            pot, poo = po_t[qi]
                            tpv = P.op("tensor", lambda t, ktp=ktp, qi=qi, pot=pot, poo=poo: t.matmul(
                                pot[:, poo:poo + VSTR],
                                EXPS[:, (ktp % 2) * 512 + qi * 128:(ktp % 2) * 512 + (qi + 1) * 128],
                                VV[:, ktp * VSTR:ktp * VSTR + VSTR],
                                start=False, stop=True),
                                waits=[("S", exp_cnt[ktp])] if qi == 0 else (),
                                inc="T" if qi == 3 else None)
                        fv = None
                        for qi in range(4):
                            pot, poo = po_t[qi]
                            fv = P.op("vector", lambda v, qi=qi, pot=pot, poo=poo: v.reciprocal(
                                STAT[4 + qi][:], pot[:, poo + 128:poo + 129]),
                                waits=[("T", tpv)] if qi == 0 else (),
                                inc="V" if qi == 3 else None)
                        for qi in range(4):
                            pot, poo = po_t[qi]
                            v_norm = P.op("vector", lambda v, qi=qi, pot=pot, poo=poo: v.tensor_scalar_mul(
                                OBUF[:, qi * 128:(qi + 1) * 128],
                                pot[:, poo:poo + 128], STAT[4 + qi][:]),
                                waits=[("V", fv)] if qi == 0 else (), inc="V")
                        ttr = None
                        for qi in range(4):
                            ttr = P.op("tensor", lambda t, qi=qi: t.transpose(
                                pT[:, qi * 128:(qi + 1) * 128], OBUF[:, qi * 128:(qi + 1) * 128],
                                IDB[:]),
                                waits=[("V", v_norm), ("V", v_ot), ("DG", 16)] if qi == 0 else (),
                                inc="T" if qi == 3 else None)
                        for qi in range(4):
                            v_ot = P.op("vector", lambda v, qi=qi, h=h, qg=qg: v.tensor_copy(
                                AB[:, h * S_Q + qg * 512 + qi * 128:h * S_Q + qg * 512 + (qi + 1) * 128],
                                pT[:, qi * 128:(qi + 1) * 128]),
                                waits=[("T", ttr)] if qi == 0 else (), inc="V")
                v_p2_done = P.cnt["V"]
                t_p2_done = P.cnt["T"]
                s_p2_done = P.cnt["S"]

                if dbg is not None and dbg[0] == "ot":
                    P.op("gpsimd", lambda g: g.dma_start(out=d_dbg[:], in_=AB[:]),
                         waits=[("V", v_p2_done)], inc="DG")
                if dbg is not None and dbg[0] in ("kt", "qh", "krb", "qrb", "vv", "exps"):
                    srcb = {"kt": KT, "qh": QH, "krb": KRB, "qrb": QRB, "vv": VV,
                            "exps": EXPS}[dbg[0]]
                    P.op("gpsimd", lambda g, srcb=srcb: g.dma_start(out=d_dbg[:], in_=srcb[:]),
                         waits=[("V", v_p2_done), ("T", t_p2_done), ("S", s_p2_done)], inc="DG")

        s2 = contextlib.ExitStack()
        with s2:
            RT = s2.enter_context(nc.sbuf_tensor("RT", [128, 16 * D], BF16))
            s2a = contextlib.ExitStack()
            with s2a:
                FB = s2a.enter_context(nc.sbuf_tensor("FB", [128, 8192], F32))
                XQ = s2a.enter_context(nc.sbuf_tensor("XQ", [128, 2 * D], F32))
                RWT = s2a.enter_context(nc.sbuf_tensor("RWT", [128, 16 * NE], F32))
                zbuf = lambda: FB[:, 0:2048]
                cbuf = lambda: FB[:, 4096:6144]
                x1tf = lambda: FB[:, 6144:8192]

                t_wo_dma = None
                for s in range(16):
                    t_wo_dma = P.op("gpsimd", lambda g, s=s: g.dma_start(
                        out=RT[:, s * D:(s + 1) * D], in_=d_wo[s * 128:(s + 1) * 128, :]),
                        waits=[("T", t_p2_done), ("V", v_p2_done), ("S", s_p2_done)]
                        if s == 0 else (), inc="DG")
                dg_wo = P.cnt["DG"]
                for s in range(16):
                    P.op("sync", lambda g, s=s: g.dma_start(
                        out=RWT[:, s * NE:(s + 1) * NE], in_=d_rwt[s * 128:(s + 1) * 128, :]),
                        waits=[("T", t_p2_done), ("V", v_p2_done)] if s == 0 else (), inc="DS")
                ds_rwt = P.cnt["DS"]

                ds_xq = {}
                v_ln = {}
                v_evz = 0
                v_x1 = 0
                s_rt = 0

                def xq_dma(t):
                    jx = t % 2
                    w = ([("V", v_ln[t - 2])] if t >= 2
                         else [("T", t_p2_done), ("V", v_p2_done)])
                    ds_xq[t] = P.op("sync", lambda g, t=t, jx=jx: g.dma_start(
                        out=XQ[:, jx * D:(jx + 1) * D], in_=d_xq[t * 128:(t + 1) * 128, :]),
                        waits=w, inc="DS")
                zb = lambda t: FB[:, (t % 2) * 2048:(t % 2) * 2048 + 2048]
                xq_dma(0)
                xq_dma(1)

                def emit_wo_ln(t):
                    nonlocal v_evz
                    jx = t % 2
                    for dch in range(4):
                        pW = pA if dch % 2 == 0 else pB
                        tv = None
                        for s in range(16):
                            w0 = []
                            if t == 0 and dch == 0 and s == 0:
                                w0 = [("DG", dg_wo)]
                            elif s == 0:
                                w0 = [("V", v_evz)]
                            tv = P.op("tensor", lambda tn, s=s, dch=dch, t=t, pW=pW: tn.matmul(
                                pW[:, 0:512],
                                AB[:, s * S_Q + t * 128:s * S_Q + (t + 1) * 128],
                                RT[:, s * D + dch * 512:s * D + (dch + 1) * 512],
                                start=(s == 0), stop=(s == 15)),
                                waits=w0, inc="T" if s == 15 else None)
                        v_evz = P.op("vector", lambda v, dch=dch, jx=jx, pW=pW, t=t: v.tensor_tensor(
                            zb(t)[:, dch * 512:(dch + 1) * 512], pW[:, 0:512],
                            XQ[:, jx * D + dch * 512:jx * D + (dch + 1) * 512], op=ALU.add),
                            waits=[("T", tv), ("DS", ds_xq[t])], inc="V")
                    if dbg is not None and dbg[0] == "z0" and t == 0:
                        P.op("sync", lambda g, t=t: g.dma_start(out=d_dbg[:], in_=zb(t)),
                             waits=[("V", v_evz)], inc="DS")
                    sv = P.op("scalar", lambda s, jx=jx, t=t: s.activation(
                        XQ[:, jx * D:(jx + 1) * D], zb(t), AF.Square,
                        accum_out=STAT[2][:]),
                        waits=[("V", v_evz)], inc="S")
                    fv = P.op("vector", lambda v, t=t: v.reduce_sum(
                        STAT[0][:], zb(t), axis=AX.X), inc="V")
                    fv = P.op("vector", lambda v: v.tensor_scalar(
                        STAT[1][:], STAT[0][:], 1.0 / D, None, op0=ALU.mult),
                        waits=[("V", fv)], inc="V")
                    vc = P.op("vector", lambda v, t=t: v.tensor_scalar_sub(
                        cbuf(), zb(t), STAT[1][:]), waits=[("V", fv)], inc="V")
                    fv = P.op("vector", lambda v: v.tensor_tensor(
                        STAT[4][:], STAT[1][:], STAT[1][:], op=ALU.mult),
                        waits=[("V", fv)], inc="V")
                    fv = P.op("vector", lambda v: v.scalar_tensor_tensor(
                        STAT[3][:], STAT[2][:], 1.0 / D, STAT[4][:],
                        op0=ALU.mult, op1=ALU.subtract),
                        waits=[("V", fv), ("S", sv)], inc="V")
                    fv = P.op("vector", lambda v: v.tensor_scalar(
                        STAT[3][:], STAT[3][:], 1.0, LN_EPS, op0=ALU.mult, op1=ALU.add),
                        waits=[("V", fv)], inc="V")
                    sv = P.op("scalar", lambda s: s.sqrt(STAT[3][:], STAT[3][:]),
                              waits=[("V", fv)], inc="S")
                    fv = P.op("vector", lambda v: v.reciprocal(STAT[3][:], STAT[3][:]),
                         waits=[("S", sv)], inc="V")
                    v_ln[t] = P.op("vector", lambda v, t=t: v.tensor_scalar_mul(
                        zb(t), cbuf(), STAT[3][:]), waits=[("V", fv)], inc="V")
                    if t + 2 < 16:
                        xq_dma(t + 2)

                tail_sx = {}
                s_rt_tile = {}

                def emit_tail(t):
                    nonlocal v_x1, s_rt
                    s_x1 = tail_sx.get(t - 1, 0)
                    for r in range(4):
                        pf, off = (pY0, 0) if r % 2 == 0 else (pY0, 512)
                        ttr = None
                        for b_ in range(4):
                            f = r * 4 + b_
                            wtr = []
                            if b_ == 0:
                                wtr = [("V", v_ln[t] if r == 0 else v_x1),
                                       ("DS", ds_setup), ("S", s_x1)]
                            ttr = P.op("tensor", lambda tn, f=f, b_=b_, pf=pf, off=off, t=t:
                                       tn.transpose(
                                pf[:, off + b_ * 128:off + (b_ + 1) * 128],
                                zb(t)[:, f * 128:(f + 1) * 128], IDF[:]),
                                waits=wtr, inc="T" if b_ == 3 else None)
                        if r % 2 == 0:
                            s_x1 = P.op("scalar", lambda s, r=r, pf=pf, off=off: s.copy(
                                x1tf()[:, r * 512:(r + 1) * 512], pf[:, off:off + 512]),
                                waits=[("T", ttr)], inc="S")
                            for b_ in range(4):
                                f = r * 4 + b_
                                s_x1 = P.op("scalar", lambda s, f=f, b_=b_, t=t, pf=pf, off=off:
                                            s.copy(
                                    AB[:, f * S_Q + t * 128:f * S_Q + (t + 1) * 128],
                                    pf[:, off + b_ * 128:off + (b_ + 1) * 128]), inc="S")
                        else:
                            v_x1 = P.op("vector", lambda v, r=r, pf=pf, off=off: v.tensor_copy(
                                x1tf()[:, r * 512:(r + 1) * 512], pf[:, off:off + 512]),
                                waits=[("T", ttr)], inc="V")
                            for b_ in range(4):
                                f = r * 4 + b_
                                v_x1 = P.op("vector", lambda v, f=f, b_=b_, t=t, pf=pf, off=off:
                                            v.tensor_copy(
                                    AB[:, f * S_Q + t * 128:f * S_Q + (t + 1) * 128],
                                    pf[:, off + b_ * 128:off + (b_ + 1) * 128]), inc="V")
                        tail_sx[t] = s_x1
                    trm = None
                    for s in range(16):
                        trm = P.op("tensor", lambda tn, s=s: tn.matmul(
                            pSM[:, 0:NE], x1tf()[:, s * 128:(s + 1) * 128],
                            RWT[:, s * NE:(s + 1) * NE], start=(s == 0), stop=(s == 15)),
                            waits=[("V", v_x1), ("DS", ds_rwt), ("S", tail_sx[t])]
                            if s == 0 else (),
                            inc="T" if s == 15 else None)
                    s_rt = P.op("scalar", lambda s, t=t: s.activation(
                        SCO[:, t * NE:(t + 1) * NE], pSM[:, 0:NE], AF.Sigmoid),
                        waits=[("T", trm)], inc="S")
                    s_rt_tile[t] = s_rt

                def topk_tile(t):
                    s_rt = s_rt_tile[t]
                    fv = P.op("vector", lambda v, t=t: v.tensor_tensor(
                        TK0[:], SCO[:, t * NE:(t + 1) * NE], EBB[:], op=ALU.add),
                        waits=[("S", s_rt), ("DS", ds_setup)], inc="V")
                    for j in range(NE):
                        P.op("vector", lambda v, j=j: v.tensor_copy(
                            BJ[j][:], TK0[:, j:j + 1]),
                            waits=[("V", fv)] if j == 0 else (),
                            inc="V" if j == NE - 1 else None)
                    fv = P.cnt["V"]
                    P.op("vector", lambda v: v.memset(TK1[:], 0.0))
                    for j in range(NE):
                        fj = P.op("vector", lambda v, j=j: v.tensor_single_scalar(
                            TK2[:], TK0[:], BJ[j][:], op=ALU.is_lt),
                            waits=[("V", fv)] if j == 0 else (), inc="V")
                        fa = P.op("vector", lambda v: v.tensor_tensor(
                            TK1[:], TK1[:], TK2[:], op=ALU.add), waits=[("V", fj)],
                            inc="V" if j == NE - 1 else None)
                    if dbg is not None and dbg[0] == "tkd" and t == 15:
                        fvx = P.cnt["V"]
                        for j in range(NE):
                            P.op("sync", lambda g, j=j: g.dma_start(
                                out=d_dbg[:, j:j + 1], in_=BJ[j][:],
                                allow_slow_non_contiguous=True),
                                waits=[("V", fvx)] if j == 0 else (), inc="DS")
                        P.op("sync", lambda g: g.dma_start(
                            out=d_dbg[:, 16:32], in_=TK1[:]), inc="DS")
                        P.op("sync", lambda g: g.dma_start(
                            out=d_dbg[:, 32:48], in_=TK0[:]), inc="DS")
                    wdbg = [("V", fa)]
                    if dbg is not None and dbg[0] == "tkd" and t == 15:
                        wdbg.append(("DS", P.cnt["DS"]))
                    fv = P.op("vector", lambda v: v.tensor_scalar(
                        TK1[:], TK1[:], 4.0, None, op0=ALU.is_lt),
                        waits=wdbg, inc="V")
                    fv = P.op("vector", lambda v, t=t: v.tensor_tensor(
                        TK1[:], TK1[:], SCO[:, t * NE:(t + 1) * NE], op=ALU.mult),
                        waits=[("V", fv)], inc="V")
                    fv = P.op("vector", lambda v: v.reduce_sum(
                        STAT[9][:], TK1[:], axis=AX.X), waits=[("V", fv)], inc="V")
                    fv = P.op("vector", lambda v: v.tensor_scalar(
                        STAT[9][:], STAT[9][:], 1.0, 1e-20, op0=ALU.mult, op1=ALU.add),
                        waits=[("V", fv)], inc="V")
                    fv = P.op("vector", lambda v: v.reciprocal(STAT[9][:], STAT[9][:]),
                              waits=[("V", fv)], inc="V")
                    P.op("vector", lambda v, t=t: v.tensor_scalar_mul(
                        CMB[:, t * NE:(t + 1) * NE], TK1[:], STAT[9][:]),
                        waits=[("V", fv)], inc="V")

                t_wo_last = 0
                for t in range(17):
                    if t >= 1:
                        emit_tail(t - 1)
                    if t < 16:
                        emit_wo_ln(t)
                        t_wo_last = P.cnt["T"]
                v_p3_tiles = P.cnt["V"]
                t_p3_tiles = P.cnt["T"]
                s_p3_tiles = P.cnt["S"]
                v_rt0 = P.op("vector", lambda v: v.memset(RT[:, 0:8 * D], 0.0),
                             waits=[("T", t_wo_last)], inc="V")
                v_rt0 = P.op("vector", lambda v: v.memset(RT[:, 8 * D:16 * D], 0.0), inc="V")

                v_p3_done = P.cnt["V"]
                t_p3_done = P.cnt["T"]
                s_p3_done = P.cnt["S"]

                if dbg is not None and dbg[0] == "x1":
                    P.op("gpsimd", lambda g: g.dma_start(out=d_dbg[:], in_=AB[:]),
                         waits=[("V", v_p3_done)], inc="DG")
                if dbg is not None and dbg[0] == "cmb":
                    P.op("sync", lambda g: g.dma_start(out=d_dbg[:], in_=CMB[:]),
                         waits=[("V", v_p3_done)], inc="DS")
                if dbg is not None and dbg[0] == "sco":
                    P.op("sync", lambda g: g.dma_start(out=d_dbg[:], in_=SCO[:]),
                         waits=[("V", v_p3_done), ("S", s_p3_done)], inc="DS")

            s2b = contextlib.ExitStack()
            with s2b:
                ACT = s2b.enter_context(nc.sbuf_tensor("ACT", [128, 6 * S_Q], BF16))
                DWN = s2b.enter_context(nc.sbuf_tensor("DWN", [128, 6 * D], BF16))
                GUP = s2b.enter_context(nc.sbuf_tensor("GUP", [128, 2 * 16 * 256], BF16))
                SIL = s2b.enter_context(nc.sbuf_tensor("SIL", [128, 512], BF16))

                gup3 = d_gupc.rearrange("e c (s p) i -> e c p s i", p=128)
                dwn3 = d_dwnt.rearrange("e (s p) i -> e p s i", p=128)

                dg_gup = {}
                dg_dwn = {}

                def gup_dma(ci, w=()):
                    e, c = divmod(ci, 6)
                    jg = ci % 2
                    for sg in range(4):
                        dg_gup[ci] = P.op("gpsimd", lambda g, e=e, c=c, jg=jg, sg=sg:
                                          g.dma_start(
                            out=GUP[:, jg * 4096 + sg * 1024:jg * 4096 + (sg + 1) * 1024],
                            in_=gup3[e, c, :, sg * 4:(sg + 1) * 4, :]),
                            waits=list(w) if sg == 0 else (), inc="DG")

                def dwn_dma(e, w=()):
                    dg_dwn[e] = P.op("gpsimd", lambda g, e=e: g.dma_start(
                        out=DWN[:], in_=dwn3[e]),
                        waits=list(w), inc="DG")

                w_region = [("T", t_p3_tiles), ("V", v_p3_tiles), ("S", s_p3_tiles)]
                gup_dma(0, w_region)
                gup_dma(1)
                dwn_dma(0, w_region)

                topk_left = list(range(16))
                t_dwn_done = {}
                v_y_bank = {0: 0, 1: 0}
                s_sil = 0
                v_mul = 0
                for e in range(NE + 1):
                    for c in range(6):
                        ci = e * 6 + c
                        jg = ci % 2
                        for n in range(4):
                            tg = None
                            for s in range(16):
                                w0 = []
                                if s == 0:
                                    w0 = [("DG", dg_gup[ci]), ("S", s_sil)]
                                    if ci == 0 and n == 0:
                                        w0.append(("V", v_rt0))
                                tg = P.op("tensor", lambda tn, s=s, n=n, jg=jg: tn.matmul(
                                    pA[:, 0:512],
                                    GUP[:, jg * 4096 + s * 256:jg * 4096 + s * 256 + 128],
                                    AB[:, s * S_Q + n * 512:s * S_Q + (n + 1) * 512],
                                    start=(s == 0), stop=(s == 15)),
                                    waits=w0, inc="T" if s == 15 else None)
                            tu = None
                            for s in range(16):
                                w0 = [("V", v_mul)] if s == 0 else ()
                                tu = P.op("tensor", lambda tn, s=s, n=n, jg=jg: tn.matmul(
                                    pB[:, 0:512],
                                    GUP[:, jg * 4096 + s * 256 + 128:jg * 4096 + (s + 1) * 256],
                                    AB[:, s * S_Q + n * 512:s * S_Q + (n + 1) * 512],
                                    start=(s == 0), stop=(s == 15)),
                                    waits=w0, inc="T" if s == 15 else None)
                            s_sil = P.op("scalar", lambda s: s.activation(
                                SIL[:], pA[:, 0:512], AF.Silu),
                                waits=[("T", tg), ("V", v_mul)], inc="S")
                            v_mul = P.op("vector", lambda v, c=c, n=n: v.tensor_tensor(
                                ACT[:, c * S_Q + n * 512:c * S_Q + (n + 1) * 512],
                                SIL[:], pB[:, 0:512], op=ALU.mult),
                                waits=[("T", tu), ("S", s_sil)], inc="V")
                            if e == 0 and topk_left:
                                topk_tile(topk_left.pop(0))
                        if ci + 2 <= (NE + 1) * 6 - 1:
                            gup_dma(ci + 2, [("T", P.cnt["T"])])
                    v_act_done = P.cnt["V"]
                    for t in range(16):
                        for half in range(2):
                            yb = (t * 2 + half) % 2
                            pYx = pY0 if yb == 0 else pY1
                            tv = None
                            for n2 in range(2):
                                for s in range(6):
                                    w0 = []
                                    if n2 == 0 and s == 0:
                                        w0 = [("V", v_y_bank[yb]), ("DG", dg_dwn[e])]
                                        if t == 0 and half == 0:
                                            w0.append(("V", v_act_done))
                                    tv = P.op("tensor", lambda tn, s=s, t=t, half=half, n2=n2, pYx=pYx:
                                              tn.matmul(
                                        pYx[:, n2 * 512:(n2 + 1) * 512],
                                        ACT[:, s * S_Q + t * 128:s * S_Q + (t + 1) * 128],
                                        DWN[:, s * D + half * 1024 + n2 * 512:
                                            s * D + half * 1024 + (n2 + 1) * 512],
                                        start=(s == 0), stop=(s == 5)),
                                        waits=w0, inc="T" if (s == 5 and n2 == 1) else None)
                            if e < NE:
                                v_y_bank[yb] = P.op("vector", lambda v, t=t, half=half, e=e, pYx=pYx:
                                           v.scalar_tensor_tensor(
                                    RT[:, t * D + half * 1024:t * D + (half + 1) * 1024],
                                    pYx[:, 0:1024], CMB[:, t * NE + e:t * NE + e + 1],
                                    RT[:, t * D + half * 1024:t * D + (half + 1) * 1024],
                                    op0=ALU.mult, op1=ALU.add),
                                    waits=[("T", tv)], inc="V")
                            else:
                                v_y_bank[yb] = P.op("vector", lambda v, t=t, half=half, pYx=pYx:
                                           v.tensor_tensor(
                                    RT[:, t * D + half * 1024:t * D + (half + 1) * 1024],
                                    pYx[:, 0:1024],
                                    RT[:, t * D + half * 1024:t * D + (half + 1) * 1024],
                                    op=ALU.add),
                                    waits=[("T", tv)], inc="V")
                    t_dwn_done[e] = P.cnt["T"]
                    if e < NE:
                        dwn_dma(e + 1, [("T", t_dwn_done[e])])
                v_p4_done = P.cnt["V"]
                t_p4_done = P.cnt["T"]

                if dbg is not None and dbg[0] == "rt":
                    P.op("gpsimd", lambda g: g.dma_start(out=d_dbg[:], in_=RT[:]),
                         waits=[("V", v_p4_done)], inc="DG")

            s2c = contextlib.ExitStack()
            with s2c:
                FB2 = s2c.enter_context(nc.sbuf_tensor("FB2", [128, 6144], F32))
                OUTB = s2c.enter_context(nc.sbuf_tensor("OUTB", [128, 2 * D], F32))
                z2 = lambda: FB2[:, 0:2048]
                c2 = lambda: FB2[:, 2048:4096]
                q2 = lambda: FB2[:, 4096:6144]

                ds_out = {}
                v_z = 0
                for t in range(16):
                    jo = t % 2
                    for r in range(2):
                        ttr = None
                        for b_ in range(8):
                            f = r * 8 + b_
                            w0 = []
                            if b_ == 0:
                                w0 = [("V", v_z), ("T", t_p4_done)]
                            ttr = P.op("tensor", lambda tn, f=f, b_=b_, t=t: tn.transpose(
                                pT[:, b_ * 128:(b_ + 1) * 128],
                                AB[:, f * S_Q + t * 128:f * S_Q + (t + 1) * 128], IDB[:]),
                                waits=w0, inc="T" if b_ == 7 else None)
                        v_z = P.op("vector", lambda v, r=r, t=t: v.tensor_tensor(
                            z2()[:, r * 1024:(r + 1) * 1024], pT[:, 0:1024],
                            RT[:, t * D + r * 1024:t * D + (r + 1) * 1024], op=ALU.add),
                            waits=[("T", ttr), ("V", v_p4_done)], inc="V")
                    fv = P.op("vector", lambda v: v.reduce_sum(
                        STAT[0][:], z2(), axis=AX.X), inc="V")
                    fv = P.op("vector", lambda v: v.tensor_scalar(
                        STAT[1][:], STAT[0][:], 1.0 / D, None, op0=ALU.mult),
                        waits=[("V", fv)], inc="V")
                    vc = P.op("vector", lambda v: v.tensor_scalar_sub(
                        c2(), z2(), STAT[1][:]), waits=[("V", fv)], inc="V")
                    sv = P.op("scalar", lambda s: s.activation(
                        q2(), c2(), AF.Square, accum_out=STAT[2][:]),
                        waits=[("V", vc)], inc="S")
                    vv = P.op("vector", lambda v: v.tensor_scalar(
                        STAT[3][:], STAT[2][:], 1.0 / D, LN_EPS, op0=ALU.mult, op1=ALU.add),
                        waits=[("S", sv)], inc="V")
                    sv = P.op("scalar", lambda s: s.sqrt(STAT[3][:], STAT[3][:]),
                              waits=[("V", vv)], inc="S")
                    fv = P.op("vector", lambda v: v.reciprocal(STAT[3][:], STAT[3][:]),
                         waits=[("S", sv)], inc="V")
                    w_ob = ([("DS", ds_out[t - 2])] if t >= 2 else []) + [("V", fv)]
                    v_ob = P.op("vector", lambda v, jo=jo: v.tensor_scalar_mul(
                        OUTB[:, jo * D:(jo + 1) * D], c2(), STAT[3][:]),
                        waits=w_ob, inc="V")
                    ds_out[t] = P.op("sync", lambda g, t=t, jo=jo: g.dma_start(
                        out=d_out[t * 128:(t + 1) * 128, :], in_=OUTB[:, jo * D:(jo + 1) * D]),
                        waits=[("V", v_ob)], inc="DS")

                P.op("sync", lambda g: g.nop(), waits=[("DS", ds_out[15])])

        with nc.Block() as block:
            @block.tensor
            def _(eng):
                P.replay("tensor", eng, sems)

            @block.vector
            def _(eng):
                P.replay("vector", eng, sems)

            @block.scalar
            def _(eng):
                P.replay("scalar", eng, sems)

            @block.gpsimd
            def _(eng):
                P.replay("gpsimd", eng, sems)

            @block.sync
            def _(eng):
                P.replay("sync", eng, sems)

    return nc


def _rope_tables(n):
    invf = (1.0 / (10000.0 ** (np.arange(0, PD, 2, dtype=np.float32) / PD))).astype(np.float32)
    t = np.arange(n, dtype=np.float32)
    th = invf[:, None] * t[None, :]
    cos = np.cos(np.concatenate([th, th], 0)).astype(np.float32)
    sin = np.sin(np.concatenate([th, th], 0)).astype(np.float32)
    sgn = np.ones((PD, 1), np.float32)
    sgn[: PD // 2] = -1.0
    return cos, (sin * sgn).astype(np.float32)


def kernel(L0_attn_hidden_states, L0_attn_W_DKV, L0_attn_W_DQ, L0_attn_W_UK,
           L0_attn_W_UV, L0_attn_W_UQ, L0_attn_W_KR, L0_attn_W_QR, L0_attn_W_O,
           L0_ffn_router_w, L0_ffn_e_bias, L0_ffn_gate_up_proj, L0_ffn_down_proj,
           L0_ffn_shared_gate_w, L0_ffn_shared_up_w, L0_ffn_shared_down_w):
    x = np.asarray(L0_attn_hidden_states, dtype=np.float32)
    asf = lambda a: np.ascontiguousarray(np.asarray(a, dtype=np.float32))

    wdkv = asf(L0_attn_W_DKV)
    wdq = asf(L0_attn_W_DQ)
    wkr = asf(L0_attn_W_KR)
    wukh = asf(np.asarray(L0_attn_W_UK, np.float32).reshape(LAT, H, HD).transpose(1, 0, 2))
    wuvh = asf(np.asarray(L0_attn_W_UV, np.float32).reshape(LAT, H, HD).transpose(1, 0, 2))
    wuqh = asf(np.asarray(L0_attn_W_UQ, np.float32).reshape(LAT, H, HD).transpose(1, 0, 2))
    wqrh = asf(np.asarray(L0_attn_W_QR, np.float32).reshape(LAT, H, PD).transpose(1, 0, 2))
    wo = asf(L0_attn_W_O)
    rwt = asf(np.asarray(L0_ffn_router_w, np.float32).T)
    ebias_bc = asf(np.broadcast_to(np.asarray(L0_ffn_e_bias, np.float32)[None, :], (128, NE)))

    gup = np.asarray(L0_ffn_gate_up_proj, np.float32)
    dwn = np.asarray(L0_ffn_down_proj, np.float32)
    sg = np.asarray(L0_ffn_shared_gate_w, np.float32)
    su = np.asarray(L0_ffn_shared_up_w, np.float32)
    sd = np.asarray(L0_ffn_shared_down_w, np.float32)

    gupc = np.empty((NE + 1, 6, D, 256), np.float32)
    for e in range(NE):
        gT = gup[e, :NI].T
        uT = gup[e, NI:].T
        for c in range(6):
            gupc[e, c, :, :128] = gT[:, c * 128:(c + 1) * 128]
            gupc[e, c, :, 128:] = uT[:, c * 128:(c + 1) * 128]
    sgT = sg.T
    suT = su.T
    for c in range(6):
        gupc[NE, c, :, :128] = sgT[:, c * 128:(c + 1) * 128]
        gupc[NE, c, :, 128:] = suT[:, c * 128:(c + 1) * 128]
    dwnt = np.empty((NE + 1, NI, D), np.float32)
    for e in range(NE):
        dwnt[e] = dwn[e].T
    dwnt[NE] = sd.T

    cosk, sink = _rope_tables(S_FULL)
    idf = np.eye(128, dtype=np.float32)

    nc = build_nc()
    in_maps = []
    for c in range(8):
        b, qh = c // 2, c % 2
        qoff = qh * S_Q
        xT = np.ascontiguousarray(x[b].T)
        xqT = np.ascontiguousarray(xT[:, qoff:qoff + S_Q])
        xq = np.ascontiguousarray(x[b, qoff:qoff + S_Q])
        in_maps.append(dict(
            xT=xT, xqT=xqT, xq=xq, wdkv=wdkv, wdq=wdq, wkr=wkr,
            wukh=wukh, wuvh=wuvh, wuqh=wuqh, wqrh=wqrh, wo=wo,
            cosk=cosk, sink=sink,
            cosq=np.ascontiguousarray(cosk[:, qoff:qoff + S_Q]),
            sinq=np.ascontiguousarray(sink[:, qoff:qoff + S_Q]),
            idf=idf, rwt=rwt, ebias_bc=ebias_bc, gupc=gupc, dwnt=dwnt))

    res = run_bass_kernel_spmd(nc, in_maps, core_ids=list(range(8)))
    out = np.empty((B, S_FULL, D), np.float32)
    for c in range(8):
        b, qh = c // 2, c % 2
        out[b, qh * S_Q:(qh + 1) * S_Q] = res.results[c]["out"]
    return out
